# version 10
# baseline (speedup 1.0000x reference)
import numpy as np
import concourse.bass as bass
import concourse.mybir as mybir
from concourse.bass_utils import run_bass_kernel_spmd

F32 = mybir.dt.float32
BF16 = mybir.dt.bfloat16
AF = mybir.ActivationFunctionType
ALU = mybir.AluOpType
AXX = mybir.AxisListType.X

DM, DI, NH, HD, DS, CH, KS = 1024, 2048, 32, 64, 64, 2176, 4
DIN = 2208
NB, SEQ, TILE, LC = 4, 8192, 1024, 256
NCK = TILE // LC
TK = TILE + 3
EPS = 1e-5

DEV = {"ntiles": 4, "dumps": (), "phases": 99}


class _H:
    __slots__ = ("eng", "i", "pool")

    def __init__(self, eng, i):
        self.eng, self.i = eng, i
        self.pool = None


class Prog:
    ENGS = ("tensor", "vector", "scalar", "qg", "qs")

    def __init__(self):
        self.ops = []
        self.res = {}
        self.qcnt = {"qg": 0, "qs": 0}

    NPOOL = {"qg": 10, "qs": 6}

    def emit(self, eng, fn, reads=(), writes=(), deps=(), inc=None):
        dep = set(d for d in deps if d is not None)
        for k in reads:
            e = self.res.get(k)
            if e and e[0] is not None:
                dep.add(e[0])
        for k in writes:
            e = self.res.get(k)
            if e:
                if e[0] is not None:
                    dep.add(e[0])
                dep.update(e[1])
        h = _H(eng, len(self.ops))
        sdep = {d for d in dep if d.eng == eng and d.eng not in ("qg", "qs")}
        dep = {d for d in dep if (d.eng != eng) or (d.eng in ("qg", "qs"))}
        if inc is None:
            inc = 16 if eng in ("qg", "qs") else 1
        if eng in ("qg", "qs"):
            if inc == 1:
                h.pool = "cc"
            else:
                h.pool = self.qcnt[eng] % self.NPOOL[eng]
                self.qcnt[eng] += 1
        else:
            h.pool = None
        self.ops.append([eng, fn, dep, inc, h, sdep])
        for k in reads:
            self.res.setdefault(k, [None, []])[1].append(h)
        for k in writes:
            self.res[k] = [h, []]
        return h

    def barrier_handles(self):
        last = {}
        for op in self.ops:
            last[op[0]] = op[4]
        return list(last.values())

    def run(self, block, sems, qpools):
        pos = {}
        cnt = {e: 0 for e in self.ENGS}
        for op in self.ops:
            pos[(op[0], op[4].i)] = cnt[op[0]]
            cnt[op[0]] += 1
        WIN = {"vector": 4, "scalar": 4, "tensor": 1}
        for op in self.ops:
            eng, h, sdep = op[0], op[4], op[5]
            mypos = pos[(eng, h.i)]
            for d in sdep:
                if mypos - pos[(d.eng, d.i)] <= WIN.get(eng, 1):
                    op[2].add(d)
        need = set()
        for op in self.ops:
            for d in op[2]:
                need.add((d.eng, d.i))

        def semid(h):
            if h.pool is None:
                return h.eng
            return (h.eng, h.pool)

        cum = {}
        inc_flag, cnt_after = {}, {}
        for op in self.ops:
            eng, inc, h = op[0], op[3], op[4]
            sid = semid(h)
            doinc = ((eng, h.i) in need) or eng in ("qg", "qs")
            if doinc:
                cum[sid] = cum.get(sid, 0) + inc
            inc_flag[(eng, h.i)] = inc if doinc else 0
            cnt_after[(eng, h.i)] = cum.get(sid, 0)
        final = dict(cum)

        def semobj(sid):
            if isinstance(sid, str):
                return sems[sid]
            return qpools[sid[0]][sid[1]] if sid[1] != "cc" else sems["cc"]

        per_eng = {e: [] for e in self.ENGS}
        for op in self.ops:
            per_eng[op[0]].append(op)

        def make_body(eng):
            ops = per_eng[eng]

            def body(e):
                waited = {}
                for _, fn, deps, inc, h, _s in ops:
                    tgt = {}
                    for d in deps:
                        sid = semid(d)
                        c = cnt_after[(d.eng, d.i)]
                        if c > tgt.get(sid, 0):
                            tgt[sid] = c
                    if h.pool is not None and inc_flag[(eng, h.i)]:
                        sid = semid(h)
                        prev = cnt_after[(eng, h.i)] - inc
                        if prev > tgt.get(sid, 0):
                            tgt[sid] = prev
                    for sid in sorted(tgt, key=repr):
                        if waited.get(sid, 0) < tgt[sid]:
                            e.wait_ge(semobj(sid), tgt[sid])
                            waited[sid] = tgt[sid]
                    ins = fn(e)
                    amt = inc_flag[(eng, h.i)]
                    if amt:
                        ins.then_inc(semobj(semid(h)), amt)
                if eng == "qs":
                    for sid in sorted(final, key=repr):
                        if final[sid] > 0 and waited.get(sid, 0) < final[sid]:
                            e.wait_ge(semobj(sid), final[sid])
            return body

        block.gpsimd(make_body("qg"))
        block.sync(make_body("qs"))
        block.tensor(make_body("tensor"))
        block.vector(make_body("vector"))
        block.scalar(make_body("scalar"))


def _merge2(g1, g2, r2):
    d1 = d2 = False
    while not (d1 and d2):
        if not d1:
            try:
                next(g1)
            except StopIteration:
                d1 = True
        for _ in range(r2):
            if not d2:
                try:
                    next(g2)
                except StopIteration:
                    d2 = True


def build_program(ntiles=4, dumps=(), phases=99):
    import contextlib
    nc = bass.Bass("TRN2", target_bir_lowering=False)
    NT = ntiles

    uin = nc.dram_tensor("uin", [NT, 8, 128, TK], F32, kind="ExternalInput").ap()
    w1t = nc.dram_tensor("w1t", [8, 128, DIN], F32, kind="ExternalInput").ap()
    wcvt = nc.dram_tensor("wcvt", [17, 128, 68 * 128], F32, kind="ExternalInput").ap()
    w2t = nc.dram_tensor("w2t", [8, 128, DI], F32, kind="ExternalInput").ap()
    w3t = nc.dram_tensor("w3t", [16, 128, DM], F32, kind="ExternalInput").ap()
    wfct = nc.dram_tensor("wfct", [32, 128, 1024], F32, kind="ExternalInput").ap()
    wpjt = nc.dram_tensor("wpjt", [32, 128, DM], F32, kind="ExternalInput").ap()
    alog = nc.dram_tensor("alog", [32, 1], F32, kind="ExternalInput").ap()
    convb = nc.dram_tensor("convb", [1, CH], F32, kind="ExternalInput").ap()
    gbrow = nc.dram_tensor("gbrow", [2, DI], F32, kind="ExternalInput").ap()
    pacol = nc.dram_tensor("pacol", [128, 1], F32, kind="ExternalInput").ap()
    parin = nc.dram_tensor("parin", [128, 1], F32, kind="ExternalInput").ap()
    identf_in = nc.dram_tensor("identf_in", [128, 128], F32, kind="ExternalInput").ap()
    identb_in = nc.dram_tensor("identb_in", [128, 128], F32, kind="ExternalInput").ap()
    m1f_in = nc.dram_tensor("m1f_in", [128, 512], F32, kind="ExternalInput").ap()
    eye32_in = nc.dram_tensor("eye32_in", [32, 32], F32, kind="ExternalInput").ap()
    ones512_in = nc.dram_tensor("ones512_in", [1, 512], F32, kind="ExternalInput").ap()
    dbig_in = nc.dram_tensor("dbig_in", [128, 128], F32, kind="ExternalInput").ap()
    uf_in = nc.dram_tensor("uf_in", [128, 512], F32, kind="ExternalInput").ap()
    out = nc.dram_tensor("out", [NT, TILE, DM], F32, kind="ExternalOutput").ap()

    CCL = 64 * 2048
    CCN = CCL + 32
    cc_in = [nc.dram_tensor(f"cc_in{r}", [CCN], BF16, kind="Internal").ap() for r in range(NT)]
    cc_out = [nc.dram_tensor(f"cc_out{r}", [2, CCN], BF16, kind="Internal").ap() for r in range(NT)]
    RG = [[0, 1], [2, 3], [4, 5], [6, 7]]

    p = Prog()
    stack = contextlib.ExitStack()
    sb = {}

    ARENA_B = 207 * 1024
    arena = stack.enter_context(nc.sbuf_tensor("arena", [128, ARENA_B // 2], BF16))

    key2reg = {}

    class Reg:
        def __init__(self, base, size):
            self.base, self.size, self.cur = base, size, 0
            self.cur_group = None
            self.last_handles = {}
            self.pending = ()
            self.absorbed = set()

        def V(self, name, P, cols, dt, off=None, group=0):
            nb = cols * (4 if dt == F32 else 2)
            o = self.cur if off is None else off
            if off is None:
                self.cur = o + (nb + 3) // 4 * 4
            assert o + nb <= self.size, f"{name}: region overflow {o}+{nb}>{self.size}"
            byt = self.base + o
            assert byt % 4 == 0
            if dt == BF16:
                ap = arena[0:P, byt // 2: byt // 2 + cols]
            else:
                ap = arena[:, byt // 2: byt // 2 + 2 * cols].bitcast(F32)[0:P, :]
            sb[name] = ap
            key2reg[name] = (self, group)
            return ap

        def reset(self):
            self.cur = 0
            return self

    _base = [0]

    def mkreg(size):
        r = Reg(_base[0], size)
        _base[0] += (size + 3) // 4 * 4
        return r

    PERM = mkreg(43 * 1024 + 256)
    G1a = mkreg(17512)
    G1b = mkreg(17512)
    G2 = mkreg(36 * 1024)
    G3 = mkreg(35328)
    G5 = mkreg(16448)
    G6 = mkreg(16896)
    G7 = mkreg(16 * 1024)
    G8 = mkreg(9 * 1024)
    assert _base[0] <= ARENA_B, _base[0]

    identf = PERM.V("identf", 128, 128, F32)
    identb = PERM.V("identb", 128, 128, BF16)
    m1f = PERM.V("m1f", 128, 512, F32)
    eye32 = PERM.V("eye32", 32, 32, F32)
    eye32b = PERM.V("eye32b", 32, 32, BF16)
    dbig = PERM.V("dbig", 128, 128, BF16)
    uf = PERM.V("uf", 128, 512, BF16)
    csA_ = [PERM.V(f"csA{i}", 32, 256, BF16) for i in range(2)]
    csB_ = [PERM.V(f"csB{i}", 32, 256, BF16) for i in range(2)]
    negcstm_ = [PERM.V(f"negcstm{i}", 128, 64, F32) for i in range(2)]
    dcolC_ = [PERM.V(f"dcolC{i}", 128, 32, F32) for i in range(2)]
    onesf = PERM.V("onesf", 32, 128, F32)
    onesrow_b = PERM.V("onesrow_b", 1, 512, BF16)
    onescol_b = PERM.V("onescol_b", 128, 1, BF16)
    gb = PERM.V("gb", 2, DI, BF16)
    convb_b = PERM.V("convb_b", 1, CH, BF16)
    alog_c = PERM.V("alog_c", 32, 1, F32)
    expal = PERM.V("expal", 32, 1, F32)
    pa = PERM.V("pa", 128, 1, F32)
    nega = PERM.V("nega", 128, 1, F32)
    par = PERM.V("par", 128, 1, F32)
    ompar = PERM.V("ompar", 128, 1, F32)
    murow = PERM.V("murow", 1, 512, F32)
    vrow = PERM.V("vrow", 1, 512, F32)
    vrowb = PERM.V("vrowb", 1, 512, BF16)
    babuf = PERM.V("babuf", 2, 512, BF16)
    browf = PERM.V("browf", 1, 512, F32)
    csfm = PERM.V("csfm", 32, 256, F32)
    cstm = PERM.V("cstm", 128, 64, F32)
    attm = PERM.V("attm", 128, 64, F32)
    dtm_f = PERM.V("dtm_f", 128, 64, F32)
    dectm = PERM.V("dectm", 128, 64, BF16)
    diagt = PERM.V("diagt", 32, 32, F32)
    eoffcol = PERM.V("eoffcol", 128, 32, F32)
    dtilecol = PERM.V("dtilecol", 128, 32, F32)
    dmixcol = PERM.V("dmixcol", 128, 32, F32)
    offsc = PERM.V("offsc", 32, 1, F32)
    offs_hist = PERM.V("offs_hist", 32, NCK + 1, F32)
    echs = PERM.V("echs", 32, 1, F32)
    dr0 = PERM.V("dr0", 32, 1, F32)
    dr1a = PERM.V("dr1a", 32, 1, F32)
    dr1b = PERM.V("dr1b", 32, 1, F32)
    dmixc = PERM.V("dmixc", 32, 1, F32)
    mucol = PERM.V("mucol", 128, 1, F32)
    m2col = PERM.V("m2col", 128, 1, F32)
    rscol = PERM.V("rscol", 128, 1, F32)
    lnt = PERM.V("lnt", 128, 512, BF16)
    rtmp = PERM.V("rtmp", 128, 512, BF16)
    htmp = PERM.V("htmp", 128, 512, BF16)
    s_out = PERM.V("s_out", 128, 2048, BF16)
    lr1a = PERM.V("lr1a", 128, 2048, BF16)
    lr1b = PERM.V("lr1b", 128, 2048, BF16)

    XW = 515
    xbc0 = G1a.V("xbc0", 128, 17 * XW, BF16, off=0)
    ybA = G1a.V("ybA", 128, 16 * 512, BF16, off=0, group=1)
    FW = 512
    y_sb = G1a.V("y_sb", 128, 8 * FW, F32, off=0, group=2)
    xbc1 = G1b.V("xbc1", 128, 17 * XW, BF16, off=0)
    w2h = G1b.V("w2h", 128, 8 * 1024, BF16, off=0, group=1)
    ybB = G1b.V("ybB", 128, 16 * 512, BF16, off=0, group=2)
    tmpb = G1b.V("tmpb", 128, 8 * FW, F32, off=0, group=3)

    def ybv(j, c0, n):
        if c0 < 512:
            return ybA[:, j * 512 + c0: j * 512 + c0 + n], ("ybA", j)
        return ybB[:, j * 512 + c0 - 512: j * 512 + c0 - 512 + n], ("ybB", j)

    cvo = G2.V("cvo", 128, 18 * TILE, BF16, off=0)
    mid_sb = G2.V("mid_sb", 128, 32 * FW, BF16, off=0, group=1)

    w1sb = G3.V("w1sb", 128, 8 * DIN, BF16, off=0)
    wcv0 = G3.V("wcv0", 128, 68 * 128, BF16, off=0, group=1)
    wcv1 = G3.V("wcv1", 128, 68 * 128, BF16, off=68 * 256, group=1)
    u2g = G3.V("u2g", 128, 16 * TILE, BF16, off=0, group=2)
    w3sb = G3.V("w3sb", 128, 16 * DM, BF16, off=0, group=3)
    memb = G3.V("memb", 128, 8 * FW, F32, off=0, group=4)
    accb = G3.V("accb", 128, 8 * FW, F32, off=8 * FW * 4, group=4)

    ub = G5.V("ub", 128, 8 * TK, BF16, off=0)
    x2tm = G5.V("x2tm", 128, 1024, F32, off=0, group=1)
    sqtm = G5.V("sqtm", 128, 1024, F32, off=4096, group=1)
    obuf = G5.V("obuf", 128, 1024, F32, off=8192, group=1)
    wfcsl = G5.V("wfcsl", 128, 2 * 1024, BF16, off=12288, group=1)

    a_t = G6.V("a_t", 32, TK, F32, off=0)
    s_loc = G6.V("s_loc", 128, 2048, BF16, off=4112)
    lr0 = G6.V("lr0", 128, 2048, BF16, off=8208)
    ecs_all = G6.V("ecs_all", 32, NCK * 256, BF16, off=12304)
    x1b = G6.V("x1b", 128, 8 * TILE, BF16, off=0, group=1)

    bdall_ = [G7.V(f"bdall{i}", 128, 2 * 2048, BF16, off=i * 8192) for i in range(2)]
    xpre = G7.V("xpre", 128, 8 * 512, BF16, off=0, group=1)

    key2reg["wcv"] = key2reg["wcv0"]
    key2reg["wfc"] = key2reg["wfcsl"]
    key2reg["mid"] = key2reg["mid_sb"]
    key2reg["y"] = key2reg["y_sb"]
    scoD0 = G8.V("scoD0", 128, 1024, BF16, off=0)
    scoD1 = G8.V("scoD1", 128, 1024, BF16, off=2048)
    msco_ = [G8.V(f"msco{i}", 128, 512, BF16, off=4096 + i * 1024) for i in range(2)]
    ctl_ = [G8.V(f"ctl{i}", 128, 512, BF16, off=6144 + i * 1024) for i in range(2)]
    btm = PERM.V("btm", 128, 128, BF16)
    scorr = PERM.V("scorr", 128, 128, BF16)
    abt = G8.V("abt", 128, 512, BF16, off=0, group=1)
    bbt = G8.V("bbt", 128, 512, BF16, off=1024, group=1)
    wpjsl = G8.V("wpjsl", 128, 2 * 512, BF16, off=2048, group=1)
    key2reg["wpj"] = key2reg["wpjsl"]

    pb = [stack.enter_context(nc.psum_tensor(f"pb{i}", [128, 512], F32)) for i in range(6)]
    pt = [stack.enter_context(nc.psum_tensor(f"pt{i}", [128, 1024], BF16)) for i in range(2)]
    sems = {e: stack.enter_context(nc.semaphore(f"sem_{e}")) for e in Prog.ENGS}
    sems["cc"] = stack.enter_context(nc.semaphore("sem_cc"))
    qpools = {q: [stack.enter_context(nc.semaphore(f"sem_{q}{i}")) for i in range(n)]
              for q, n in Prog.NPOOL.items()}
    block = stack.enter_context(nc.Block())

    def _wrap(eng):
        def em(fn, reads=(), writes=(), deps=(), inc=None):
            deps = list(deps)
            touched = []
            for k in tuple(reads) + tuple(writes):
                base = k if isinstance(k, str) else k[0]
                rg = key2reg.get(base)
                if rg is None:
                    continue
                r, g = rg
                if r.cur_group != g:
                    r.pending = tuple(r.last_handles.values())
                    r.absorbed = set()
                    r.last_handles = {}
                    r.cur_group = g
                if r.pending and eng not in r.absorbed:
                    deps.extend(r.pending)
                    r.absorbed.add(eng)
                touched.append(r)
            h = p.emit(eng, fn, reads=reads, writes=writes, deps=deps, inc=inc)
            hk = eng if h.pool is None else (eng, h.pool)
            for r in touched:
                r.last_handles[hk] = h
            return h
        return em

    qg, qs, te, ve, se = (_wrap(e) for e in ("qg", "qs", "tensor", "vector", "scalar"))

    def phase():
        pass

    qs(lambda e: e.dma_start(out=identf[:], in_=identf_in[:]), writes=["identf"])
    qg(lambda e: e.dma_start(out=identb[:], in_=identb_in[:]), writes=["identb"])
    qs(lambda e: e.dma_start(out=m1f[:], in_=m1f_in[:]), writes=["m1f"])
    qs(lambda e: e.dma_start(out=eye32[:], in_=eye32_in[:]), writes=["eye32"])
    qg(lambda e: e.dma_start(out=eye32b[:], in_=eye32_in[:]), writes=["eye32b"])
    qg(lambda e: e.dma_start(out=dbig[:], in_=dbig_in[:]), writes=["dbig"])
    qg(lambda e: e.dma_start(out=uf[:], in_=uf_in[:]), writes=["uf"])
    qg(lambda e: e.dma_start(out=gb[:], in_=gbrow[:]), writes=["gb"])
    qg(lambda e: e.dma_start(out=convb_b[:], in_=convb[:]), writes=["convb_b"])
    qs(lambda e: e.dma_start(out=alog_c[:], in_=alog[:]), writes=["alog_c"])
    qs(lambda e: e.dma_start(out=pa[:], in_=pacol[:]), writes=["pa"])
    qs(lambda e: e.dma_start(out=par[:], in_=parin[:]), writes=["par"])

    ve(lambda e: e.memset(onesf[:], 1.0), writes=["onesf"])
    ve(lambda e: e.memset(onesrow_b[:], 1.0), writes=["onesrow_b"])
    ve(lambda e: e.memset(onescol_b[:], 1.0), writes=["onescol_b"])
    qg(lambda e: e.dma_start(out=babuf[1:2, :], in_=ones512_in[:]), writes=["babuf1"])
    se(lambda e: e.activation(expal[:], alog_c[:], AF.Exp), reads=["alog_c"], writes=["expal"])
    ve(lambda e: e.tensor_single_scalar(nega[:], pa[:], -1.0, ALU.mult), reads=["pa"], writes=["nega"])
    ve(lambda e: e.tensor_scalar(ompar[:], par[:], -1.0, 1.0, ALU.mult, ALU.add),
       reads=["par"], writes=["ompar"])
    ve(lambda e: e.memset(lr1a[0:64, :], 0.0), writes=["lr1a"])
    ve(lambda e: e.memset(dr1a[:], 0.0), writes=["dr1a"])
    ve(lambda e: e.memset(s_out[0:64, :], 0.0), writes=["s_out"])
    ve(lambda e: e.memset(scoD0[:], 0.0), writes=["scoD0"])
    ve(lambda e: e.memset(scoD1[:], 0.0), writes=["scoD1"])

    def ln_colmath(pbS, kS, pbQ, kQ, n):
        ve(lambda e: e.tensor_single_scalar(murow[:], pbS[0:1, 0:512], 1.0 / n, ALU.mult),
           reads=[kS], writes=["murow"])
        ve(lambda e: e.tensor_single_scalar(vrow[:], pbQ[0:1, 0:512], 1.0 / n, ALU.mult),
           reads=[kQ], writes=["vrow"])
        ve(lambda e: e.tensor_tensor(browf[:], murow[:], murow[:], ALU.mult),
           reads=["murow"], writes=["browf"])
        ve(lambda e: e.tensor_tensor(vrow[:], vrow[:], browf[:], ALU.subtract),
           reads=["vrow", "browf"], writes=["vrow"])
        ve(lambda e: e.tensor_single_scalar(vrow[:], vrow[:], EPS, ALU.add),
           reads=["vrow"], writes=["vrow"])
        se(lambda e: e.activation(vrow[:], vrow[:], AF.Ln),
           reads=["vrow"], writes=["vrow"])
        se(lambda e: e.activation(vrow[:], vrow[:], AF.Exp, scale=-0.5),
           reads=["vrow"], writes=["vrow"])
        ve(lambda e: e.tensor_tensor(browf[:], murow[:], vrow[:], ALU.mult),
           reads=["murow", "vrow"], writes=["browf"])
        ve(lambda e: e.tensor_single_scalar(babuf[0:1, :], browf[:], -1.0, ALU.mult),
           reads=["browf"], writes=["babuf0"])
        ve(lambda e: e.tensor_copy(vrowb[:], vrow[:]), reads=["vrow"], writes=["vrowb"])

    def bcast64(col_t, col_key, dst, dst_key):
        ve(lambda e: e.tensor_single_scalar(diagt[:], eye32[:], col_t, ALU.mult),
           reads=["eye32", col_key], writes=["diagt"])
        te(lambda e: e.matmul(pb[2][0:64, 0:32], onesf[:, 0:64], diagt[:], start=True, stop=True),
           reads=["onesf", "diagt"], writes=[("pb", 2)])
        ve(lambda e: e.tensor_copy(dst[0:64, :], pb[2][0:64, 0:32]),
           reads=[("pb", 2)], writes=[dst_key])

    for t in range(min(NT, 99)):
        phase()
        for k in range(8):
            qg(lambda e, t=t, k=k: e.dma_start(out=ub[:, k * TK:(k + 1) * TK], in_=uin[t, k]),
               writes=[("ub", k)])
            qg(lambda e, k=k: e.dma_start(out=w1sb[:, k * DIN:(k + 1) * DIN], in_=w1t[k]),
               writes=[("w1sb", k)])
        if phases < 1:
            continue

        phase()
        wlist = [(0, 512), (512, 512), (1024, TK - 1024)]
        for f in range(18):
            M = 128 if f < 17 else 32
            for wi, (w0, wn) in enumerate(wlist):
                unit = f * 3 + wi
                bank, pk = pb[unit % 4], ("pb", unit % 4)
                for k in range(8):
                    te(lambda e, f=f, k=k, w0=w0, wn=wn, M=M, bank=bank:
                       e.matmul(bank[0:M, 0:wn],
                                w1sb[:, k * DIN + f * 128: k * DIN + f * 128 + M],
                                ub[:, k * TK + w0: k * TK + w0 + wn],
                                start=(k == 0), stop=(k == 7)),
                       reads=[("w1sb", k), ("ub", k)], writes=[pk])
                if f < 17:
                    if wi == 0:
                        se(lambda e, f=f, wn=wn, bank=bank:
                           e.activation(xbc0[:, f * XW: f * XW + wn],
                                        bank[:, 0:wn], AF.Identity),
                           reads=[pk], writes=[("xbc0", f)])
                    elif wi == 1:
                        se(lambda e, f=f, wn=wn, bank=bank:
                           e.activation(xbc1[:, f * XW: f * XW + wn],
                                        bank[:, 0:wn], AF.Identity),
                           reads=[pk], writes=[("xbc1", f)])
                        se(lambda e, f=f, bank=bank:
                           e.activation(xbc0[:, f * XW + 512: f * XW + 515],
                                        bank[:, 0:3], AF.Identity),
                           reads=[pk], writes=[("xbc0", f)])
                    else:
                        se(lambda e, f=f, wn=wn, bank=bank:
                           e.activation(xbc1[:, f * XW + 512: f * XW + 512 + wn],
                                        bank[:, 0:wn], AF.Identity),
                           reads=[pk], writes=[("xbc1", f)])
                else:
                    se(lambda e, w0=w0, wn=wn, bank=bank:
                       e.activation(a_t[:, w0: w0 + wn], bank[0:32, 0:wn], AF.Sigmoid, scale=-1.0),
                       reads=[pk], writes=["a_t"])
                    ve(lambda e, w0=w0, wn=wn:
                       e.tensor_single_scalar(a_t[:, w0: w0 + wn], a_t[:, w0: w0 + wn],
                                              1e-37, ALU.max),
                       reads=["a_t"], writes=["a_t"])
                    se(lambda e, w0=w0, wn=wn:
                       e.activation(a_t[:, w0: w0 + wn], a_t[:, w0: w0 + wn], AF.Ln),
                       reads=["a_t"], writes=["a_t"])
                    ve(lambda e, w0=w0, wn=wn:
                       e.tensor_single_scalar(a_t[:, w0: w0 + wn], a_t[:, w0: w0 + wn],
                                              expal[:], ALU.mult),
                       reads=["a_t", "expal"], writes=["a_t"])
        if phases < 2:
            continue

        phase()
        def conv_pass(w, sub=10):
            xb = xbc0 if w == 0 else xbc1
            xkey = "xbc0" if w == 0 else "xbc1"
            for oc in range(17):
                wk = ("wcv", oc % 2)
                wbuf = wcv0 if oc % 2 == 0 else wcv1
                qg(lambda e, oc=oc, wbuf=wbuf: e.dma_start(out=wbuf[:], in_=wcvt[oc]),
                   writes=[wk])
                groups = [(0, 128, oc)] if oc < 16 else [(0, 64, 16), (64, 64, 17)]
                for gi, (m0, M, dst) in enumerate(groups):
                    bi = (oc + gi) % 4 if w == 0 else (oc + gi) % 2
                    bank, pk = pb[bi], ("pb", bi)
                    for idx in range(68):
                        k, ic = idx // 17, idx % 17
                        te(lambda e, k=k, ic=ic, m0=m0, M=M, idx=idx, bank=bank, wbuf=wbuf, xb=xb:
                           e.matmul(bank[0:M, 0:512],
                                    wbuf[:, (k * 17 + ic) * 128 + m0:
                                         (k * 17 + ic) * 128 + m0 + M],
                                    xb[:, ic * XW + k: ic * XW + k + 512],
                                    start=(idx == 0), stop=False),
                           reads=[wk, (xkey, ic)], writes=[pk])
                        if idx % sub == sub - 1:
                            yield
                    te(lambda e, oc=oc, m0=m0, M=M, bank=bank:
                       e.matmul(bank[0:M, 0:512],
                                convb_b[0:1, oc * 128 + m0: oc * 128 + m0 + M],
                                onesrow_b[0:1, 0:512], start=False, stop=True),
                       reads=["convb_b", "onesrow_b"], writes=[pk])
                    se(lambda e, M=M, bank=bank:
                       e.activation(rtmp[0:M, :], bank[0:M, :], AF.Relu),
                       reads=[pk], writes=["rtmp"])
                    se(lambda e, M=M, bank=bank:
                       e.activation(htmp[0:M, :], bank[0:M, :], AF.Relu, scale=-1.0),
                       reads=[pk], writes=["htmp"])
                    ve(lambda e, dst=dst, w=w, M=M:
                       e.scalar_tensor_tensor(cvo[0:M, dst * TILE + w * 512: dst * TILE + w * 512 + 512],
                                              htmp[0:M, :], nega[0:M, :], rtmp[0:M, :],
                                              ALU.mult, ALU.add),
                       reads=["htmp", "rtmp", "nega"], writes=[("cvo", dst, w)])
                    yield
        for _ in conv_pass(0):
            pass
        if phases < 3:
            continue

        def u2_emit():
            for half in range(2):
                for k in range(8):
                    qg(lambda e, k=k, half=half:
                       e.dma_start(out=w2h[:, k * 1024: k * 1024 + 1024],
                                   in_=w2t[k, :, half * 1024: half * 1024 + 1024]),
                       writes=[("w2h", k)])
                for wi in range(2):
                    w0 = wi * 512
                    for f8 in range(8):
                        f = half * 8 + f8
                        unit = wi * 8 + f8
                        bank, pk = pb[unit % 4], ("pb", unit % 4)
                        for k in range(8):
                            te(lambda e, f8=f8, k=k, w0=w0, bank=bank:
                               e.matmul(bank[:, 0:512],
                                        w2h[:, k * 1024 + f8 * 128: k * 1024 + f8 * 128 + 128],
                                        ub[:, k * TK + 3 + w0: k * TK + 3 + w0 + 512],
                                        start=(k == 0), stop=(k == 7)),
                               reads=[("w2h", k), ("ub", k)], writes=[pk])
                        se(lambda e, f=f, w0=w0, bank=bank:
                           e.activation(u2g[:, f * TILE + w0: f * TILE + w0 + 512], bank[:], AF.Gelu),
                           reads=[pk], writes=[("u2g", f)])
        if phases < 4:
            continue

        phase()
        ve(lambda e: e.memset(s_loc[0:64, :], 0.0), writes=["s_loc"])
        ve(lambda e: e.memset(offsc[:], 0.0), writes=["offsc"])

        def ssd_chunk(c):
            tok = c * LC
            wv = c // 2
            cp = c % 2
            csA, csB = csA_[cp], csB_[cp]
            negcstm, dcolC = negcstm_[cp], dcolC_[cp]
            bdall, msco = bdall_[cp], msco_[cp]
            kcsA, kcsB = ("csA", cp), ("csB", cp)
            kneg, kdcol = ("negcstm", cp), ("dcolC", cp)
            kbd, kms = ("bdall", cp), ("msco", cp)
            for sc in range(2):
                te(lambda e, tok=tok, sc=sc:
                   e.transpose(pb[5][0:128, 0:32],
                               a_t[:, 3 + tok + sc * 128: 3 + tok + sc * 128 + 128],
                               identf[0:32, 0:32]),
                   reads=["a_t", "identf"], writes=[("pb", 5)])
                ve(lambda e, sc=sc: e.tensor_copy(attm[:, sc * 32: sc * 32 + 32], pb[5][:, 0:32]),
                   reads=[("pb", 5)], writes=["attm"])
            for sc in range(2):
                te(lambda e, sc=sc:
                   e.matmul(pb[5][0:32, 0:256], attm[:, sc * 32: sc * 32 + 32],
                            m1f[:, sc * 256: sc * 256 + 256], start=(sc == 0), stop=(sc == 1)),
                   reads=["attm", "m1f"], writes=[("pb", 5)])
            se(lambda e: e.activation(csfm[:], pb[5][0:32, 0:256], AF.Identity),
               reads=[("pb", 5)], writes=["csfm"])
            se(lambda e, c=c: e.activation(ecs_all[:, c * 256:(c + 1) * 256], csfm[:], AF.Exp),
               reads=["csfm"], writes=["ecs_all"])
            for sc in range(2):
                te(lambda e, sc=sc:
                   e.transpose(pb[5][0:128, 0:32], csfm[:, sc * 128: sc * 128 + 128],
                               identf[0:32, 0:32]),
                   reads=["csfm", "identf"], writes=[("pb", 5)])
                ve(lambda e, sc=sc: e.tensor_copy(cstm[:, sc * 32: sc * 32 + 32], pb[5][:, 0:32]),
                   reads=[("pb", 5)], writes=["cstm"])
            ve(lambda e, negcstm=negcstm: e.tensor_single_scalar(negcstm[:], cstm[:], -1.0, ALU.mult),
               reads=["cstm"], writes=[kneg])
            ve(lambda e, csA=csA: e.tensor_copy(csA[:], csfm[:]), reads=["csfm"], writes=[kcsA])
            ve(lambda e, csA=csA, csB=csB: e.tensor_tensor(csB[:], csfm[:], csA[:], ALU.subtract),
               reads=["csfm", kcsA], writes=[kcsB])
            ve(lambda e, c=c: e.tensor_copy(offs_hist[:, c:c + 1], offsc[:]),
               reads=["offsc"], writes=["offs_hist"])
            ve(lambda e: e.tensor_tensor(offsc[:], offsc[:], csfm[:, 255:256], ALU.add),
               reads=["offsc", "csfm"], writes=["offsc"])
            se(lambda e: e.activation(echs[:], csfm[:, 255:256], AF.Exp),
               reads=["csfm"], writes=["echs"])
            bcast64(echs[:], "echs", dcolC, kdcol)
            ve(lambda e: e.tensor_single_scalar(diagt[:], eye32[:], csfm[:, 255:256], ALU.mult),
               reads=["eye32", "csfm"], writes=["diagt"])
            te(lambda e: e.matmul(pb[5][0:128, 0:32], onesf[:, 0:128], diagt[:],
                                  start=True, stop=True),
               reads=["onesf", "diagt"], writes=[("pb", 5)])
            for sc in range(2):
                ve(lambda e, sc=sc:
                   e.tensor_tensor(dtm_f[:, sc * 32: sc * 32 + 32], pb[5][:, 0:32],
                                   cstm[:, sc * 32: sc * 32 + 32], ALU.subtract),
                   reads=[("pb", 5), "cstm"], writes=["dtm_f"])
            se(lambda e: e.activation(dectm[:], dtm_f[:], AF.Exp),
               reads=["dtm_f"], writes=["dectm"])
            yield

            for f in range(16):
                for sc in range(2):
                    col = f * TILE + tok + sc * 128
                    ptb, ptk = pt[sc], ("pt", sc)
                    te(lambda e, col=col, ptb=ptb:
                       e.transpose(ptb[:, 0:128], cvo[:, col: col + 128], identb[:]),
                       reads=[("cvo", f, wv), "identb"], writes=[ptk])
                    ve(lambda e, col=col, ptb=ptb: e.tensor_copy(cvo[:, col: col + 128],
                                                                 ptb[:, 0:128]),
                       reads=[ptk], writes=[("cvo", f, wv)])
                if f % 4 == 3:
                    yield
            for sc in range(2):
                col = 16 * TILE + tok + sc * 128
                te(lambda e, col=col: e.transpose(pt[0][0:128, 0:64], cvo[0:64, col: col + 128],
                                                  identb[0:64, 0:64]),
                   reads=[("cvo", 16, wv), "identb"], writes=[("pt", 0)])
                ve(lambda e, sc=sc: e.tensor_copy(btm[:, sc * 64: sc * 64 + 64], pt[0][:, 0:64]),
                   reads=[("pt", 0)], writes=["btm"])
            for sc in range(2):
                ve(lambda e, sc=sc, bdall=bdall:
                   e.tensor_tensor(
                       bdall[:, sc * 2048:(sc + 1) * 2048].rearrange("p (h n) -> p h n", n=64),
                       btm[:, sc * 64: sc * 64 + 64].unsqueeze(1).broadcast_to([128, 32, 64]),
                       dectm[:, sc * 32: sc * 32 + 32].unsqueeze(2).broadcast_to([128, 32, 64]),
                       ALU.mult),
                   reads=["btm", "dectm"], writes=[kbd])

            for sc in range(2):
                te(lambda e, tok=tok, sc=sc:
                   e.matmul(pb[5][:, sc * 256: sc * 256 + 256],
                            cvo[0:64, 16 * TILE + tok + sc * 128: 16 * TILE + tok + sc * 128 + 128],
                            cvo[0:64, 17 * TILE + tok: 17 * TILE + tok + 256],
                            start=True, stop=True),
                   reads=[("cvo", 16, wv), ("cvo", 17, wv)], writes=[("pb", 5)])
            ve(lambda e, msco=msco: e.tensor_copy(msco[:], pb[5][:, 0:512]),
               reads=[("pb", 5)], writes=[kms])
            yield

            for j in range(16):
                par2 = j % 2
                pY = pb[4]
                pYk = ("pb", 4)
                if c < 2:
                    bA, kA = pb[2], ("pb", 2)
                    bB, kB = pb[3], ("pb", 3)
                else:
                    bA, kA = pb[0 + 2 * par2], ("pb", 0 + 2 * par2)
                    bB, kB = pb[1 + 2 * par2], ("pb", 1 + 2 * par2)
                scoD = scoD0 if par2 == 0 else scoD1
                skD = "scoD0" if par2 == 0 else "scoD1"
                ctl, kctl = ctl_[par2], ("ctl", par2)
                for hh in range(2):
                    h = 2 * j + hh
                    for sc, l0, ln in ((0, 0, 256), (1, 128, 128)):
                        bank, kk = (bA, kA) if sc == 0 else (bB, kB)
                        te(lambda e, h=h, hh=hh, l0=l0, ln=ln, bank=bank, csA=csA:
                           e.matmul(bank[:, hh * 256: hh * 256 + ln],
                                    eye32b[:, h: h + 1].broadcast_to([32, 128]),
                                    csA[:, l0: l0 + ln], start=True, stop=False),
                           reads=["eye32b", kcsA], writes=[kk])
                        te(lambda e, h=h, hh=hh, l0=l0, ln=ln, bank=bank, csB=csB:
                           e.matmul(bank[:, hh * 256: hh * 256 + ln],
                                    eye32b[:, h: h + 1].broadcast_to([32, 128]),
                                    csB[:, l0: l0 + ln], start=False, stop=False),
                           reads=["eye32b", kcsB], writes=[kk])
                        te(lambda e, sc=sc, hh=hh, l0=l0, ln=ln, bank=bank:
                           e.matmul(bank[:, hh * 256: hh * 256 + ln], dbig[:],
                                    uf[:, sc * 256 + l0: sc * 256 + l0 + ln],
                                    start=False, stop=True),
                           reads=["dbig", "uf"], writes=[kk])
                for hh in range(2):
                    h = 2 * j + hh
                    for sc, l0, ln in ((0, 0, 256), (1, 128, 128)):
                        bank, kk = (bA, kA) if sc == 0 else (bB, kB)
                        se(lambda e, h=h, hh=hh, sc=sc, l0=l0, ln=ln, bank=bank, scoD=scoD, negcstm=negcstm:
                           e.activation(scoD[:, hh * 512 + sc * 256 + l0:
                                             hh * 512 + sc * 256 + l0 + ln],
                                        bank[:, hh * 256: hh * 256 + ln], AF.Exp,
                                        bias=negcstm[:, sc * 32 + h: sc * 32 + h + 1]),
                           reads=[kk, kneg], writes=[skD])
                ve(lambda e, scoD=scoD, msco=msco:
                   e.tensor_tensor(scoD[:].rearrange("p (h x) -> p h x", h=2),
                                   scoD[:].rearrange("p (h x) -> p h x", h=2),
                                   msco[:].unsqueeze(1).broadcast_to([128, 2, 512]),
                                   ALU.mult),
                   reads=[skD, kms], writes=[skD])
                for hh in range(2):
                    h = 2 * j + hh
                    te(lambda e, h=h, c=c, bB=bB:
                       e.matmul(bB[0:64, 0:256],
                                eye32b[:, h: h + 1].broadcast_to([32, 64]),
                                ecs_all[:, c * 256:(c + 1) * 256], start=True, stop=True),
                       reads=["eye32b", "ecs_all"], writes=[kB])
                    ve(lambda e, hh=hh, tok=tok, bB=bB, ctl=ctl:
                       e.tensor_tensor(ctl[0:64, hh * 256: hh * 256 + 256],
                                       cvo[0:64, 17 * TILE + tok: 17 * TILE + tok + 256],
                                       bB[0:64, 0:256], ALU.mult),
                       reads=[("cvo", 17, wv), kB], writes=[kctl])
                for hh in range(2):
                    h = 2 * j + hh
                    xcol = (h // 2) * TILE + tok
                    for sc, l0, ln in ((0, 0, 256), (1, 128, 128)):
                        te(lambda e, h=h, hh=hh, sc=sc, l0=l0, ln=ln, xcol=xcol, pY=pY, scoD=scoD:
                           e.matmul(pY[hh * 64: hh * 64 + 64, l0: l0 + ln],
                                    cvo[:, xcol + sc * 128 + (h % 2) * 64:
                                        xcol + sc * 128 + (h % 2) * 64 + 64],
                                    scoD[:, hh * 512 + sc * 256 + l0:
                                         hh * 512 + sc * 256 + l0 + ln],
                                    start=(sc == 0), stop=False),
                           reads=[("cvo", h // 2, wv), skD], writes=[pYk])
                    te(lambda e, h=h, hh=hh, pY=pY, ctl=ctl:
                       e.matmul(pY[hh * 64: hh * 64 + 64, 0:256],
                                s_loc[0:64, h * 64: h * 64 + 64],
                                ctl[0:64, hh * 256: hh * 256 + 256], start=False, stop=True),
                       reads=["s_loc", kctl], writes=[pYk])
                for hh in range(2):
                    h = 2 * j + hh
                    xcol = (h // 2) * TILE + tok
                    for sc in range(2):
                        te(lambda e, h=h, hh=hh, sc=sc, xcol=xcol, pY=pY, bdall=bdall:
                           e.matmul(pY[0:64, 256 + hh * 64: 256 + hh * 64 + 64],
                                    bdall[:, sc * 2048 + h * 64: sc * 2048 + h * 64 + 64],
                                    cvo[:, xcol + sc * 128 + (h % 2) * 64:
                                        xcol + sc * 128 + (h % 2) * 64 + 64],
                                    start=(sc == 0), stop=(sc == 1)),
                           reads=[kbd, ("cvo", h // 2, wv)], writes=[pYk])
                for hh in range(2):
                    h = 2 * j + hh
                    ve(lambda e, h=h, hh=hh, pY=pY, dcolC=dcolC:
                       e.scalar_tensor_tensor(s_loc[0:64, h * 64: h * 64 + 64],
                                              s_loc[0:64, h * 64: h * 64 + 64],
                                              dcolC[0:64, h: h + 1],
                                              pY[0:64, 256 + hh * 64: 256 + hh * 64 + 64],
                                              ALU.mult, ALU.add),
                       reads=[kdcol, pYk, "s_loc"], writes=["s_loc"])
                ydst, ydk = ybv(j, tok, 256)
                ve(lambda e, ydst=ydst, pY=pY: e.tensor_copy(ydst, pY[:, 0:256]),
                   reads=[pYk], writes=[ydk])
                yield

        def ssd01():
            for c in (0, 1):
                yield from ssd_chunk(c)
        _merge2(ssd01(), conv_pass(1), 3)
        u2_emit()
        for c in (2, 3):
            for _ in ssd_chunk(c):
                pass
        if phases < 5:
            continue

        phase()
        lr1_cur, lr1_prev = (lr1b, lr1a) if t % 2 == 0 else (lr1a, lr1b)
        dr1_cur, dr1_prev = (dr1b, dr1a) if t % 2 == 0 else (dr1a, dr1b)
        k_lr1c, k_lr1p = ("lr1b", "lr1a") if t % 2 == 0 else ("lr1a", "lr1b")
        k_dr1c, k_dr1p = ("dr1b", "dr1a") if t % 2 == 0 else ("dr1a", "dr1b")

        se(lambda e: e.activation(offs_hist[:, NCK:NCK + 1], offsc[:], AF.Exp),
           reads=["offsc"], writes=["offs_hist"])
        qg(lambda e, t=t: e.dma_start(out=cc_in[t][0:CCL].rearrange("(p n) -> p n", p=64),
                                      in_=s_loc[0:64, :]),
           reads=["s_loc"], writes=[("ccin", t)])
        qg(lambda e, t=t: e.dma_start(out=cc_in[t][CCL:CCN].rearrange("(p n) -> p n", p=32),
                                      in_=offs_hist[:, NCK:NCK + 1]),
           reads=["offs_hist"], writes=[("ccin2", t)])
        p.emit("qg", lambda e, t=t: e.collective_compute(
            "AllGather", ALU.bypass, ins=[cc_in[t][:]], outs=[cc_out[t][:]],
            replica_groups=RG),
            reads=[("ccin", t), ("ccin2", t)], writes=[("ccout", t)], inc=1)
        qg(lambda e, t=t: e.dma_start(
            out=lr0[0:64, :], in_=cc_out[t][0, 0:CCL].rearrange("(p n) -> p n", p=64)),
            reads=[("ccout", t)], writes=["lr0"])
        qg(lambda e, t=t, lr1_cur=lr1_cur: e.dma_start(
            out=lr1_cur[0:64, :], in_=cc_out[t][1, 0:CCL].rearrange("(p n) -> p n", p=64)),
            reads=[("ccout", t)], writes=[k_lr1c])
        qg(lambda e, t=t: e.dma_start(
            out=dr0[:], in_=cc_out[t][0, CCL:CCN].rearrange("(p n) -> p n", p=32)),
            reads=[("ccout", t)], writes=["dr0"])
        qg(lambda e, t=t, dr1_cur=dr1_cur: e.dma_start(
            out=dr1_cur[:], in_=cc_out[t][1, CCL:CCN].rearrange("(p n) -> p n", p=32)),
            reads=[("ccout", t)], writes=[k_dr1c])

        ve(lambda e: e.tensor_single_scalar(dmixc[:], dr0[:], par[0:32, :], ALU.mult),
           reads=["dr0", "par"], writes=["dmixc"])
        ve(lambda e, dr1_prev=dr1_prev:
           e.scalar_tensor_tensor(dmixc[:], dr1_prev[:], ompar[0:32, :], dmixc[:],
                                  ALU.mult, ALU.add),
           reads=[k_dr1p, "ompar", "dmixc"], writes=["dmixc"])
        ve(lambda e: e.tensor_single_scalar(lr0[0:64, :], lr0[0:64, :], par[0:64, :],
                                            ALU.mult),
           reads=["lr0", "par"], writes=["lr0"])
        ve(lambda e, lr1_prev=lr1_prev:
           e.scalar_tensor_tensor(lr0[0:64, :], lr1_prev[0:64, :], ompar[0:64, :],
                                  lr0[0:64, :], ALU.mult, ALU.add),
           reads=[k_lr1p, "ompar", "lr0"], writes=["lr0"])
        bcast64(dmixc[:], "dmixc", dmixcol, "dmixcol")
        bcast64(offs_hist[:, NCK:NCK + 1], "offs_hist", dtilecol, "dtilecol")
        for h in range(NH):
            ve(lambda e, h=h:
               e.scalar_tensor_tensor(lr0[0:64, h * 64: h * 64 + 64],
                                      s_out[0:64, h * 64: h * 64 + 64],
                                      dmixcol[0:64, h: h + 1],
                                      lr0[0:64, h * 64: h * 64 + 64], ALU.mult, ALU.add),
               reads=["s_out", "dmixcol", "lr0"], writes=["lr0"])
        for h in range(NH):
            ve(lambda e, h=h:
               e.scalar_tensor_tensor(s_out[0:64, h * 64: h * 64 + 64],
                                      lr0[0:64, h * 64: h * 64 + 64],
                                      dtilecol[0:64, h: h + 1],
                                      s_loc[0:64, h * 64: h * 64 + 64], ALU.mult, ALU.add),
               reads=["lr0", "dtilecol", "s_loc"], writes=["s_out"])

        phase()
        for c in range(NCK):
            tok = c * LC
            se(lambda e, c=c: e.activation(echs[:], offs_hist[:, c:c + 1], AF.Exp),
               reads=["offs_hist"], writes=["echs"])
            bcast64(echs[:], "echs", eoffcol, "eoffcol")
            wv = c // 2
            for j in range(16):
                pY = pb[5] if j % 2 == 0 else pb[4]
                pYk = ("pb", 5) if j % 2 == 0 else ("pb", 4)
                ctl, kctl = ctl_[j % 2], ("ctl", j % 2)
                for hh in range(2):
                    h = 2 * j + hh
                    ve(lambda e, h=h, hh=hh:
                       e.tensor_single_scalar(scorr[0:64, hh * 64: hh * 64 + 64],
                                              lr0[0:64, h * 64: h * 64 + 64],
                                              eoffcol[0:64, h: h + 1], ALU.mult),
                       reads=["lr0", "eoffcol"], writes=["scorr"])
                    te(lambda e, h=h, c=c:
                       e.matmul(pb[3][0:64, 0:256],
                                eye32b[:, h: h + 1].broadcast_to([32, 64]),
                                ecs_all[:, c * 256:(c + 1) * 256], start=True, stop=True),
                       reads=["eye32b", "ecs_all"], writes=[("pb", 3)])
                    ve(lambda e, hh=hh, tok=tok, ctl=ctl:
                       e.tensor_tensor(ctl[0:64, hh * 256: hh * 256 + 256],
                                       cvo[0:64, 17 * TILE + tok: 17 * TILE + tok + 256],
                                       pb[3][0:64, 0:256], ALU.mult),
                       reads=[("cvo", 17, wv), ("pb", 3)], writes=[kctl])
                for hh in range(2):
                    te(lambda e, hh=hh, pY=pY, ctl=ctl:
                       e.matmul(pY[hh * 64: hh * 64 + 64, 0:256],
                                scorr[0:64, hh * 64: hh * 64 + 64],
                                ctl[0:64, hh * 256: hh * 256 + 256], start=True, stop=True),
                       reads=["scorr", kctl], writes=[pYk])
                ydst, ydk = ybv(j, tok, 256)
                ve(lambda e, pY=pY, ydst=ydst:
                   e.tensor_tensor(lnt[:, 0:256], pY[:, 0:256], ydst, ALU.add),
                   reads=[pYk, ydk], writes=["lnt"])
                ve(lambda e, j=j, tok=tok, ydst=ydst:
                   e.tensor_tensor(ydst, lnt[:, 0:256],
                                   u2g[:, j * TILE + tok: j * TILE + tok + 256], ALU.mult),
                   reads=["lnt", ("u2g", j)], writes=[ydk])
        if phases < 6:
            continue

        phase()
        for k in range(16):
            qg(lambda e, k=k: e.dma_start(out=w3sb[:, k * DM:(k + 1) * DM], in_=w3t[k]),
               writes=[("w3sb", k)])
        for wi in range(2):
            w0 = wi * 512
            for f in range(16):
                yw, ywk = ybv(f, w0, 512)
                se(lambda e, yw=yw:
                   e.activation(htmp[:], yw, AF.Square),
                   reads=[ywk], writes=["htmp"])
                te(lambda e, f=f, yw=yw:
                   e.matmul(pb[0][0:1, 0:512], onescol_b[:, 0:1], yw,
                            start=(f == 0), stop=(f == 15)),
                   reads=[ywk, "onescol_b"], writes=[("pb", 0)])
                te(lambda e, f=f:
                   e.matmul(pb[1][0:1, 0:512], onescol_b[:, 0:1], htmp[:],
                            start=(f == 0), stop=(f == 15)),
                   reads=["htmp", "onescol_b"], writes=[("pb", 1)])
            ln_colmath(pb[0], ("pb", 0), pb[1], ("pb", 1), DI)
            for f in range(16):
                te(lambda e, f=f: e.matmul(pb[2][:, 0:512], gb[0:1, f * 128: f * 128 + 128],
                                           vrowb[0:1, 0:512], start=True, stop=True),
                   reads=["gb", "vrowb"], writes=[("pb", 2)])
                te(lambda e, f=f: e.matmul(pb[3][:, 0:512], gb[:, f * 128: f * 128 + 128],
                                           babuf[:, 0:512], start=True, stop=True),
                   reads=["gb", "babuf0", "babuf1"], writes=[("pb", 3)])
                ve(lambda e: e.tensor_copy(abt[:], pb[2][:, 0:512]),
                   reads=[("pb", 2)], writes=["abt"])
                ve(lambda e: e.tensor_copy(bbt[:], pb[3][:, 0:512]),
                   reads=[("pb", 3)], writes=["bbt"])
                yw, ywk = ybv(f, w0, 512)
                ve(lambda e, yw=yw:
                   e.tensor_tensor(lnt[:], yw, abt[:], ALU.mult),
                   reads=[ywk, "abt"], writes=["lnt"])
                ve(lambda e, yw=yw:
                   e.tensor_tensor(yw, lnt[:], bbt[:], ALU.add),
                   reads=["lnt", "bbt"], writes=[ywk])

            for d in range(8):
                bank, pk = pb[4 + (d % 2)], ("pb", 4 + (d % 2))
                for k in range(16):
                    yw, ywk = ybv(k, w0, 512)
                    te(lambda e, d=d, k=k, yw=yw, bank=bank:
                       e.matmul(bank[:, 0:512], w3sb[:, k * DM + d * 128: k * DM + d * 128 + 128],
                                yw, start=(k == 0), stop=(k == 15)),
                       reads=[("w3sb", k), ywk], writes=[pk])
                ve(lambda e, d=d, w0=w0, bank=bank:
                   e.tensor_tensor(xpre[:, d * 512: d * 512 + 512], bank[:, 0:512],
                                   ub[:, d * TK + 3 + w0: d * TK + 3 + w0 + 512], ALU.add),
                   reads=[pk, ("ub", d)], writes=[("xpre", d)])
            for d in range(8):
                se(lambda e, d=d: e.activation(htmp[:], xpre[:, d * 512: d * 512 + 512], AF.Square),
                   reads=[("xpre", d)], writes=["htmp"])
                te(lambda e, d=d: e.matmul(pb[0][0:1, 0:512], onescol_b[:, 0:1],
                                           xpre[:, d * 512: d * 512 + 512],
                                           start=(d == 0), stop=(d == 7)),
                   reads=[("xpre", d), "onescol_b"], writes=[("pb", 0)])
                te(lambda e, d=d: e.matmul(pb[1][0:1, 0:512], onescol_b[:, 0:1], htmp[:],
                                           start=(d == 0), stop=(d == 7)),
                   reads=["htmp", "onescol_b"], writes=[("pb", 1)])
            ln_colmath(pb[0], ("pb", 0), pb[1], ("pb", 1), DM)
            te(lambda e: e.matmul(pb[2][:, 0:512], onesrow_b[0:1, 0:128], vrowb[0:1, 0:512],
                                  start=True, stop=True),
               reads=["onesrow_b", "vrowb"], writes=[("pb", 2)])
            te(lambda e: e.matmul(pb[3][:, 0:512], onesrow_b[0:1, 0:128], babuf[0:1, 0:512],
                                  start=True, stop=True),
               reads=["onesrow_b", "babuf0"], writes=[("pb", 3)])
            ve(lambda e: e.tensor_copy(abt[:], pb[2][:, 0:512]), reads=[("pb", 2)], writes=["abt"])
            ve(lambda e: e.tensor_copy(bbt[:], pb[3][:, 0:512]), reads=[("pb", 3)], writes=["bbt"])
            for d in range(8):
                ve(lambda e, d=d:
                   e.tensor_tensor(lnt[:], xpre[:, d * 512: d * 512 + 512], abt[:], ALU.mult),
                   reads=[("xpre", d), "abt"], writes=["lnt"])
                ve(lambda e, d=d, w0=w0:
                   e.tensor_tensor(x1b[:, d * TILE + w0: d * TILE + w0 + 512], lnt[:], bbt[:],
                                   ALU.add),
                   reads=["lnt", "bbt"], writes=[("x1b", d)])
        if phases < 7:
            continue

        phase()
        for hi in range(2):
            ht0 = hi * 512
            for j in range(32):
                slot = j % 2
                qg(lambda e, j=j, slot=slot:
                   e.dma_start(out=wfcsl[:, slot * 1024: slot * 1024 + 1024], in_=wfct[j]),
                   writes=[("wfc", slot)])
                bank, pk = pb[(2 + j) % 4 if False else (j % 2)], ("pb", j % 2)
                for k in range(8):
                    te(lambda e, j=j, k=k, slot=slot, ht0=ht0, bank=bank:
                       e.matmul(bank[:, 0:512],
                                wfcsl[:, slot * 1024 + k * 128: slot * 1024 + k * 128 + 128],
                                x1b[:, k * TILE + ht0: k * TILE + ht0 + 512],
                                start=(k == 0), stop=(k == 7)),
                       reads=[("wfc", slot), ("x1b", k)], writes=[pk])
                se(lambda e, bank=bank: e.activation(rtmp[:], bank[:], AF.Relu),
                   reads=[pk], writes=["rtmp"])
                ve(lambda e, j=j: e.tensor_tensor(mid_sb[:, j * FW: j * FW + FW], rtmp[:],
                                                  rtmp[:], ALU.mult),
                   reads=["rtmp"], writes=[("mid", j)])
            for dq in range(4):
                for j in range(32):
                    slot = j % 2
                    qg(lambda e, j=j, dq=dq, slot=slot:
                       e.dma_start(out=wpjsl[:, slot * 256: slot * 256 + 256],
                                   in_=wpjt[j, :, dq * 256: dq * 256 + 256]),
                       writes=[("wpj", slot)])
                    for w2i in range(2):
                        for dd in range(2):
                            bk = 2 + w2i * 2 + dd
                            te(lambda e, j=j, slot=slot, w2i=w2i, dd=dd, bk=bk:
                               e.matmul(pb[bk][:, 0:256],
                                        wpjsl[:, slot * 256 + dd * 128: slot * 256 + dd * 128 + 128],
                                        mid_sb[:, j * FW + w2i * 256: j * FW + w2i * 256 + 256],
                                        start=(j == 0), stop=(j == 31)),
                               reads=[("wpj", slot), ("mid", j)], writes=[("pb", bk)])
                for w2i in range(2):
                    for dd in range(2):
                        d = dq * 2 + dd
                        bk = 2 + w2i * 2 + dd
                        ve(lambda e, d=d, bk=bk, w2i=w2i:
                           e.tensor_copy(y_sb[:, d * FW + w2i * 256: d * FW + w2i * 256 + 256],
                                         pb[bk][:, 0:256]),
                           reads=[("pb", bk)], writes=[("y", d)])
            yk = [("y", d) for d in range(8)]
            ve(lambda e: e.tensor_copy(memb[:], y_sb[:]), reads=yk, writes=["memb"])
            ve(lambda e: e.tensor_single_scalar(accb[:], y_sb[:], 1.0, ALU.is_gt),
               reads=yk, writes=["accb"])
            ve(lambda e: e.tensor_single_scalar(y_sb[:], y_sb[:], 1.0, ALU.subtract),
               reads=yk, writes=yk)
            for it in range(4):
                ve(lambda e: e.scalar_tensor_tensor(tmpb[:], memb[:], 0.9, y_sb[:],
                                                    ALU.mult, ALU.add),
                   reads=["memb"] + yk, writes=["tmpb"])
                ve(lambda e: e.scalar_tensor_tensor(memb[:], memb[:], 1.0, tmpb[:],
                                                    ALU.is_le, ALU.add),
                   reads=["memb", "tmpb"], writes=["memb"])
                ve(lambda e: e.scalar_tensor_tensor(accb[:], memb[:], 1.0, accb[:],
                                                    ALU.is_gt, ALU.add),
                   reads=["memb", "accb"], writes=["accb"])
            for d in range(8):
                ve(lambda e, d=d, ht0=ht0:
                   e.scalar_tensor_tensor(tmpb[:, d * FW: d * FW + FW],
                                          accb[:, d * FW: d * FW + FW], 0.2,
                                          x1b[:, d * TILE + ht0: d * TILE + ht0 + 512],
                                          ALU.mult, ALU.add),
                   reads=["accb", ("x1b", d)], writes=["tmpb"])
            for tg in range(4):
                for d in range(8):
                    te(lambda e, d=d, tg=tg:
                       e.transpose(pb[0][:, 0:128], tmpb[:, d * FW + tg * 128: d * FW + tg * 128 + 128],
                                   identf[:]),
                       reads=["tmpb", "identf"], writes=[("pb", 0)])
                    ve(lambda e, d=d: e.tensor_copy(x2tm[:, d * 128: d * 128 + 128],
                                                    pb[0][:, 0:128]),
                       reads=[("pb", 0)], writes=["x2tm"])
                ve(lambda e: e.reduce_sum(mucol[:], x2tm[:], axis=AXX),
                   reads=["x2tm"], writes=["mucol"])
                se(lambda e: e.activation(sqtm[:], x2tm[:], AF.Square),
                   reads=["x2tm"], writes=["sqtm"])
                ve(lambda e: e.reduce_sum(m2col[:], sqtm[:], axis=AXX),
                   reads=["sqtm"], writes=["m2col"])
                ve(lambda e: e.tensor_single_scalar(mucol[:], mucol[:], 1.0 / DM, ALU.mult),
                   reads=["mucol"], writes=["mucol"])
                ve(lambda e: e.tensor_single_scalar(m2col[:], m2col[:], 1.0 / DM, ALU.mult),
                   reads=["m2col"], writes=["m2col"])
                ve(lambda e: e.tensor_tensor(rscol[:], mucol[:], mucol[:], ALU.mult),
                   reads=["mucol"], writes=["rscol"])
                ve(lambda e: e.tensor_tensor(m2col[:], m2col[:], rscol[:], ALU.subtract),
                   reads=["m2col", "rscol"], writes=["m2col"])
                ve(lambda e: e.tensor_single_scalar(m2col[:], m2col[:], EPS, ALU.add),
                   reads=["m2col"], writes=["m2col"])
                se(lambda e: e.activation(m2col[:], m2col[:], AF.Ln),
                   reads=["m2col"], writes=["m2col"])
                se(lambda e: e.activation(rscol[:], m2col[:], AF.Exp, scale=-0.5),
                   reads=["m2col"], writes=["rscol"])
                ve(lambda e: e.tensor_scalar(obuf[:], x2tm[:], mucol[:], rscol[:],
                                             ALU.subtract, ALU.mult),
                   reads=["x2tm", "mucol", "rscol"], writes=["obuf"])
                qs(lambda e, t=t, hi=hi, tg=tg:
                   e.dma_start(out=out[t, hi * 512 + tg * 128: hi * 512 + tg * 128 + 128, :],
                               in_=obuf[:]),
                   reads=["obuf"])

    bar = p.barrier_handles()
    for name in dumps:
        tsr = sb[name]
        ap0 = tsr[:]
        od = nc.dram_tensor("dbg_" + name, list(ap0.shape), F32, kind="ExternalOutput").ap()
        qg(lambda e, tsr=tsr, od=od: e.dma_start(out=od[:], in_=tsr[:]), deps=bar)

    p.run(block, sems, qpools)
    stack.close()
    return nc


def prep_shared(inputs):
    f = np.ascontiguousarray
    W1 = inputs["L0_attn_W_in1"]
    w1t = f(np.asarray(W1)[DI:DI + DIN, :].T).reshape(8, 128, DIN)
    cw = np.asarray(inputs["L0_attn_conv_w"])
    cwt = cw.reshape(17, 128, 17, 128, KS)
    cwt = np.transpose(cwt, (0, 3, 4, 2, 1))
    wcvt = f(cwt).reshape(17, 128, 68 * 128)
    w2t = f(np.asarray(inputs["L0_attn_W_in2"]).T).reshape(8, 128, DI)
    w3t = f(np.asarray(inputs["L0_attn_W_in3"]).T).reshape(16, 128, DM)
    Wfc = np.asarray(inputs["L0_ffn_W_fc"])
    wfct = f(np.transpose(Wfc.reshape(32, 128, 8, 128), (0, 3, 2, 1))).reshape(32, 128, 1024)
    Wpj = np.asarray(inputs["L0_ffn_W_proj"])
    wpjt = f(Wpj.T).reshape(32, 128, DM)

    m1 = np.tril(np.ones((256, 256), np.float32)).T
    m1f = f(np.transpose(m1.reshape(2, 128, 256), (1, 0, 2))).reshape(128, 512)

    return {
        "w1t": w1t.astype(np.float32),
        "wcvt": wcvt.astype(np.float32),
        "w2t": w2t.astype(np.float32),
        "w3t": w3t.astype(np.float32),
        "wfct": wfct.astype(np.float32),
        "wpjt": wpjt.astype(np.float32),
        "alog": np.asarray(inputs["L0_attn_A_log"]).reshape(32, 1).astype(np.float32),
        "convb": np.asarray(inputs["L0_attn_conv_b"]).reshape(1, CH).astype(np.float32),
        "gbrow": np.stack([np.asarray(inputs["L0_attn_ln_g"]),
                           np.asarray(inputs["L0_attn_ln_b"])]).astype(np.float32),
        "pacol": np.full((128, 1), np.asarray(inputs["L0_attn_prelu_a"]), np.float32),
        "identf_in": np.eye(128, dtype=np.float32),
        "identb_in": np.eye(128, dtype=np.float32),
        "m1f_in": m1f.astype(np.float32),
        "eye32_in": np.eye(32, dtype=np.float32),
        "ones512_in": np.ones((1, 512), np.float32),
        "dbig_in": (-65536.0 * np.eye(128)).astype(np.float32),
        "uf_in": (1.0 - m1f).astype(np.float32),
    }


def prep_core(u, core, ntiles):
    b, parity = core // 2, core % 2
    ut = np.ascontiguousarray(np.asarray(u)[b].T)
    tiles = []
    for tt in range(ntiles):
        g = 2 * tt + parity
        t0 = g * TILE
        if t0 == 0:
            seg = np.concatenate([np.zeros((DM, 3), np.float32), ut[:, 0:TILE]], axis=1)
        else:
            seg = ut[:, t0 - 3: t0 + TILE]
        tiles.append(seg)
    arr = np.stack(tiles).reshape(ntiles, 8, 128, TK)
    return np.ascontiguousarray(arr.astype(np.float32))


def kernel(**inputs):
    NT = 4
    shared = prep_shared(inputs)
    u = inputs["L0_attn_u"]
    in_maps = []
    for core in range(8):
        m = dict(shared)
        m["uin"] = prep_core(u, core, NT)
        m["parin"] = np.full((128, 1), float(core % 2), np.float32)
        in_maps.append(m)
    nc = build_program(ntiles=NT, dumps=(), phases=99)
    res = run_bass_kernel_spmd(nc, in_maps, core_ids=list(range(8)))
    y = np.empty((NB, SEQ, DM), np.float32)
    for core in range(8):
        b, parity = core // 2, core % 2
        o = res.results[core]["out"]
        for tt in range(NT):
            g = 2 * tt + parity
            y[b, g * TILE:(g + 1) * TILE, :] = o[tt]
    return y


# version 11 (final)
# speedup vs baseline: 1.3850x; 1.3850x over previous
import numpy as np
import concourse.bass as bass
import concourse.mybir as mybir
from concourse.bass_utils import run_bass_kernel_spmd

F32 = mybir.dt.float32
BF16 = mybir.dt.bfloat16
AF = mybir.ActivationFunctionType
ALU = mybir.AluOpType
AXX = mybir.AxisListType.X

DM, DI, NH, HD, DS, CH, KS = 1024, 2048, 32, 64, 64, 2176, 4
DIN = 2208
NB, SEQ, TILE, LC = 4, 8192, 1024, 256
NCK = TILE // LC
TK = TILE + 3
EPS = 1e-5

DEV = {"ntiles": 4, "dumps": (), "phases": 99}


class _H:
    __slots__ = ("eng", "i", "pool")

    def __init__(self, eng, i):
        self.eng, self.i = eng, i
        self.pool = None


class Prog:
    ENGS = ("tensor", "vector", "scalar", "qg", "qs")

    def __init__(self):
        self.ops = []
        self.res = {}
        self.qcnt = {"qg": 0, "qs": 0}

    NPOOL = {"qg": 10, "qs": 6}

    def emit(self, eng, fn, reads=(), writes=(), deps=(), inc=None):
        dep = set(d for d in deps if d is not None)
        for k in reads:
            e = self.res.get(k)
            if e and e[0] is not None:
                dep.add(e[0])
        for k in writes:
            e = self.res.get(k)
            if e:
                if e[0] is not None:
                    dep.add(e[0])
                dep.update(e[1])
        h = _H(eng, len(self.ops))
        sdep = {d for d in dep if d.eng == eng and d.eng not in ("qg", "qs")}
        dep = {d for d in dep if (d.eng != eng) or (d.eng in ("qg", "qs"))}
        if inc is None:
            inc = 16 if eng in ("qg", "qs") else 1
        if eng in ("qg", "qs"):
            if inc == 1:
                h.pool = "cc"
            else:
                h.pool = self.qcnt[eng] % self.NPOOL[eng]
                self.qcnt[eng] += 1
        else:
            h.pool = None
        self.ops.append([eng, fn, dep, inc, h, sdep])
        for k in reads:
            self.res.setdefault(k, [None, []])[1].append(h)
        for k in writes:
            self.res[k] = [h, []]
        return h

    def barrier_handles(self):
        last = {}
        for op in self.ops:
            last[op[0]] = op[4]
        return list(last.values())

    def run(self, block, sems, qpools):
        pos = {}
        cnt = {e: 0 for e in self.ENGS}
        for op in self.ops:
            pos[(op[0], op[4].i)] = cnt[op[0]]
            cnt[op[0]] += 1
        WIN = {"vector": 4, "scalar": 4, "tensor": 1}
        for op in self.ops:
            eng, h, sdep = op[0], op[4], op[5]
            mypos = pos[(eng, h.i)]
            for d in sdep:
                if mypos - pos[(d.eng, d.i)] <= WIN.get(eng, 1):
                    op[2].add(d)
        need = set()
        for op in self.ops:
            for d in op[2]:
                need.add((d.eng, d.i))

        def semid(h):
            if h.pool is None:
                return h.eng
            return (h.eng, h.pool)

        cum = {}
        inc_flag, cnt_after = {}, {}
        for op in self.ops:
            eng, inc, h = op[0], op[3], op[4]
            sid = semid(h)
            doinc = ((eng, h.i) in need) or eng in ("qg", "qs")
            if doinc:
                cum[sid] = cum.get(sid, 0) + inc
            inc_flag[(eng, h.i)] = inc if doinc else 0
            cnt_after[(eng, h.i)] = cum.get(sid, 0)
        final = dict(cum)

        def semobj(sid):
            if isinstance(sid, str):
                return sems[sid]
            return qpools[sid[0]][sid[1]] if sid[1] != "cc" else sems["cc"]

        per_eng = {e: [] for e in self.ENGS}
        for op in self.ops:
            per_eng[op[0]].append(op)

        def make_body(eng):
            ops = per_eng[eng]

            def body(e):
                waited = {}
                for _, fn, deps, inc, h, _s in ops:
                    tgt = {}
                    for d in deps:
                        sid = semid(d)
                        c = cnt_after[(d.eng, d.i)]
                        if c > tgt.get(sid, 0):
                            tgt[sid] = c
                    if h.pool is not None and inc_flag[(eng, h.i)]:
                        sid = semid(h)
                        prev = cnt_after[(eng, h.i)] - inc
                        if prev > tgt.get(sid, 0):
                            tgt[sid] = prev
                    for sid in sorted(tgt, key=repr):
                        if waited.get(sid, 0) < tgt[sid]:
                            e.wait_ge(semobj(sid), tgt[sid])
                            waited[sid] = tgt[sid]
                    ins = fn(e)
                    amt = inc_flag[(eng, h.i)]
                    if amt:
                        ins.then_inc(semobj(semid(h)), amt)
                if eng == "qs":
                    for sid in sorted(final, key=repr):
                        if final[sid] > 0 and waited.get(sid, 0) < final[sid]:
                            e.wait_ge(semobj(sid), final[sid])
            return body

        block.gpsimd(make_body("qg"))
        block.sync(make_body("qs"))
        block.tensor(make_body("tensor"))
        block.vector(make_body("vector"))
        block.scalar(make_body("scalar"))


def build_program(ntiles=4, dumps=(), phases=99):
    import contextlib
    nc = bass.Bass("TRN2", target_bir_lowering=False)
    NT = ntiles

    uin = nc.dram_tensor("uin", [NT, 8, 128, TK], F32, kind="ExternalInput").ap()
    w1t = nc.dram_tensor("w1t", [8, 128, DIN], F32, kind="ExternalInput").ap()
    wcvt = nc.dram_tensor("wcvt", [17, 128, 68 * 128], F32, kind="ExternalInput").ap()
    w2t = nc.dram_tensor("w2t", [8, 128, DI], F32, kind="ExternalInput").ap()
    w3t = nc.dram_tensor("w3t", [16, 128, DM], F32, kind="ExternalInput").ap()
    wfct = nc.dram_tensor("wfct", [32, 128, 1024], F32, kind="ExternalInput").ap()
    wpjt = nc.dram_tensor("wpjt", [32, 128, DM], F32, kind="ExternalInput").ap()
    alog = nc.dram_tensor("alog", [32, 1], F32, kind="ExternalInput").ap()
    convb = nc.dram_tensor("convb", [1, CH], F32, kind="ExternalInput").ap()
    gbrow = nc.dram_tensor("gbrow", [2, DI], F32, kind="ExternalInput").ap()
    pacol = nc.dram_tensor("pacol", [128, 1], F32, kind="ExternalInput").ap()
    parin = nc.dram_tensor("parin", [128, 1], F32, kind="ExternalInput").ap()
    identf_in = nc.dram_tensor("identf_in", [128, 128], F32, kind="ExternalInput").ap()
    identb_in = nc.dram_tensor("identb_in", [128, 128], F32, kind="ExternalInput").ap()
    m1f_in = nc.dram_tensor("m1f_in", [128, 512], F32, kind="ExternalInput").ap()
    eye32_in = nc.dram_tensor("eye32_in", [32, 32], F32, kind="ExternalInput").ap()
    ones512_in = nc.dram_tensor("ones512_in", [1, 512], F32, kind="ExternalInput").ap()
    dbig_in = nc.dram_tensor("dbig_in", [128, 128], F32, kind="ExternalInput").ap()
    uf_in = nc.dram_tensor("uf_in", [128, 512], F32, kind="ExternalInput").ap()
    out = nc.dram_tensor("out", [NT, TILE, DM], F32, kind="ExternalOutput").ap()

    CCL = 64 * 2048
    CCN = CCL + 32
    cc_in = [nc.dram_tensor(f"cc_in{r}", [CCN], BF16, kind="Internal").ap() for r in range(NT)]
    cc_out = [nc.dram_tensor(f"cc_out{r}", [2, CCN], BF16, kind="Internal").ap() for r in range(NT)]
    RG = [[0, 1], [2, 3], [4, 5], [6, 7]]

    p = Prog()
    stack = contextlib.ExitStack()
    sb = {}

    ARENA_B = 207 * 1024
    arena = stack.enter_context(nc.sbuf_tensor("arena", [128, ARENA_B // 2], BF16))

    key2reg = {}

    class Reg:
        def __init__(self, base, size):
            self.base, self.size, self.cur = base, size, 0
            self.cur_group = None
            self.last_handles = {}
            self.pending = ()
            self.absorbed = set()

        def V(self, name, P, cols, dt, off=None, group=0):
            nb = cols * (4 if dt == F32 else 2)
            o = self.cur if off is None else off
            if off is None:
                self.cur = o + (nb + 3) // 4 * 4
            assert o + nb <= self.size, f"{name}: region overflow {o}+{nb}>{self.size}"
            byt = self.base + o
            assert byt % 4 == 0
            if dt == BF16:
                ap = arena[0:P, byt // 2: byt // 2 + cols]
            else:
                ap = arena[:, byt // 2: byt // 2 + 2 * cols].bitcast(F32)[0:P, :]
            sb[name] = ap
            key2reg[name] = (self, group)
            return ap

        def reset(self):
            self.cur = 0
            return self

    _base = [0]

    def mkreg(size):
        r = Reg(_base[0], size)
        _base[0] += (size + 3) // 4 * 4
        return r

    PERM = mkreg(43 * 1024 + 256)
    G1 = mkreg(35072)
    G2 = mkreg(36 * 1024)
    G3 = mkreg(35328)
    G5 = mkreg(16448)
    G6 = mkreg(16896)
    G7 = mkreg(16 * 1024)
    G8 = mkreg(9 * 1024)
    assert _base[0] <= ARENA_B, _base[0]

    identf = PERM.V("identf", 128, 128, F32)
    identb = PERM.V("identb", 128, 128, BF16)
    m1f = PERM.V("m1f", 128, 512, F32)
    eye32 = PERM.V("eye32", 32, 32, F32)
    eye32b = PERM.V("eye32b", 32, 32, BF16)
    dbig = PERM.V("dbig", 128, 128, BF16)
    uf = PERM.V("uf", 128, 512, BF16)
    csA_ = [PERM.V(f"csA{i}", 32, 256, BF16) for i in range(2)]
    csB_ = [PERM.V(f"csB{i}", 32, 256, BF16) for i in range(2)]
    negcstm_ = [PERM.V(f"negcstm{i}", 128, 64, F32) for i in range(2)]
    dcolC_ = [PERM.V(f"dcolC{i}", 128, 32, F32) for i in range(2)]
    onesf = PERM.V("onesf", 32, 128, F32)
    onesrow_b = PERM.V("onesrow_b", 1, 512, BF16)
    onescol_b = PERM.V("onescol_b", 128, 1, BF16)
    gb = PERM.V("gb", 2, DI, BF16)
    convb_b = PERM.V("convb_b", 1, CH, BF16)
    alog_c = PERM.V("alog_c", 32, 1, F32)
    expal = PERM.V("expal", 32, 1, F32)
    pa = PERM.V("pa", 128, 1, F32)
    nega = PERM.V("nega", 128, 1, F32)
    par = PERM.V("par", 128, 1, F32)
    ompar = PERM.V("ompar", 128, 1, F32)
    murow = PERM.V("murow", 1, 512, F32)
    vrow = PERM.V("vrow", 1, 512, F32)
    vrowb = PERM.V("vrowb", 1, 512, BF16)
    babuf = PERM.V("babuf", 2, 512, BF16)
    browf = PERM.V("browf", 1, 512, F32)
    csfm = PERM.V("csfm", 32, 256, F32)
    cstm = PERM.V("cstm", 128, 64, F32)
    attm = PERM.V("attm", 128, 64, F32)
    dtm_f = PERM.V("dtm_f", 128, 64, F32)
    dectm = PERM.V("dectm", 128, 64, BF16)
    diagt = PERM.V("diagt", 32, 32, F32)
    eoffcol = PERM.V("eoffcol", 128, 32, F32)
    dtilecol = PERM.V("dtilecol", 128, 32, F32)
    dmixcol = PERM.V("dmixcol", 128, 32, F32)
    offsc = PERM.V("offsc", 32, 1, F32)
    offs_hist = PERM.V("offs_hist", 32, NCK + 1, F32)
    echs = PERM.V("echs", 32, 1, F32)
    dr0 = PERM.V("dr0", 32, 1, F32)
    dr1a = PERM.V("dr1a", 32, 1, F32)
    dr1b = PERM.V("dr1b", 32, 1, F32)
    dmixc = PERM.V("dmixc", 32, 1, F32)
    mucol = PERM.V("mucol", 128, 1, F32)
    m2col = PERM.V("m2col", 128, 1, F32)
    rscol = PERM.V("rscol", 128, 1, F32)
    lnt = PERM.V("lnt", 128, 512, BF16)
    rtmp = PERM.V("rtmp", 128, 512, BF16)
    htmp = PERM.V("htmp", 128, 512, BF16)
    s_out = PERM.V("s_out", 128, 2048, BF16)
    lr1a = PERM.V("lr1a", 128, 2048, BF16)
    lr1b = PERM.V("lr1b", 128, 2048, BF16)

    xbc = G1.V("xbc", 128, 17 * TK, BF16, off=0)
    w2sb = G1.V("w2sb", 128, 8 * DI, BF16, off=0, group=1)
    yb = G1.V("yb", 128, 16 * TILE, BF16, off=0, group=2)
    FW = 512
    y_sb = G1.V("y_sb", 128, 8 * FW, F32, off=0, group=3)
    tmpb = G1.V("tmpb", 128, 8 * FW, F32, off=8 * FW * 4, group=3)

    cvo = G2.V("cvo", 128, 18 * TILE, BF16, off=0)
    mid_sb = G2.V("mid_sb", 128, 32 * FW, BF16, off=0, group=1)

    w1sb = G3.V("w1sb", 128, 8 * DIN, BF16, off=0)
    wcv0 = G3.V("wcv0", 128, 68 * 128, BF16, off=0, group=1)
    wcv1 = G3.V("wcv1", 128, 68 * 128, BF16, off=68 * 256, group=1)
    u2g = G3.V("u2g", 128, 16 * TILE, BF16, off=0, group=2)
    w3sb = G3.V("w3sb", 128, 16 * DM, BF16, off=0, group=3)
    memb = G3.V("memb", 128, 8 * FW, F32, off=0, group=4)
    accb = G3.V("accb", 128, 8 * FW, F32, off=8 * FW * 4, group=4)

    ub = G5.V("ub", 128, 8 * TK, BF16, off=0)
    x2tm = G5.V("x2tm", 128, 1024, F32, off=0, group=1)
    sqtm = G5.V("sqtm", 128, 1024, F32, off=4096, group=1)
    obuf = G5.V("obuf", 128, 1024, F32, off=8192, group=1)
    wfcsl = G5.V("wfcsl", 128, 2 * 1024, BF16, off=12288, group=1)

    a_t = G6.V("a_t", 32, TK, F32, off=0)
    s_loc = G6.V("s_loc", 128, 2048, BF16, off=4112)
    lr0 = G6.V("lr0", 128, 2048, BF16, off=8208)
    ecs_all = G6.V("ecs_all", 32, NCK * 256, BF16, off=12304)
    x1b = G6.V("x1b", 128, 8 * TILE, BF16, off=0, group=1)

    bdall_ = [G7.V(f"bdall{i}", 128, 2 * 2048, BF16, off=i * 8192) for i in range(2)]
    xpre = G7.V("xpre", 128, 8 * 512, BF16, off=0, group=1)

    key2reg["wcv"] = key2reg["wcv0"]
    key2reg["wfc"] = key2reg["wfcsl"]
    key2reg["mid"] = key2reg["mid_sb"]
    key2reg["y"] = key2reg["y_sb"]
    scoD0 = G8.V("scoD0", 128, 1024, BF16, off=0)
    scoD1 = G8.V("scoD1", 128, 1024, BF16, off=2048)
    msco_ = [G8.V(f"msco{i}", 128, 512, BF16, off=4096 + i * 1024) for i in range(2)]
    ctl_ = [G8.V(f"ctl{i}", 128, 512, BF16, off=6144 + i * 1024) for i in range(2)]
    btm = PERM.V("btm", 128, 128, BF16)
    scorr = PERM.V("scorr", 128, 128, BF16)
    abt = G8.V("abt", 128, 512, BF16, off=0, group=1)
    bbt = G8.V("bbt", 128, 512, BF16, off=1024, group=1)
    wpjsl = G8.V("wpjsl", 128, 2 * 512, BF16, off=2048, group=1)
    key2reg["wpj"] = key2reg["wpjsl"]

    pb = [stack.enter_context(nc.psum_tensor(f"pb{i}", [128, 512], F32)) for i in range(6)]
    pt = [stack.enter_context(nc.psum_tensor(f"pt{i}", [128, 1024], BF16)) for i in range(2)]
    sems = {e: stack.enter_context(nc.semaphore(f"sem_{e}")) for e in Prog.ENGS}
    sems["cc"] = stack.enter_context(nc.semaphore("sem_cc"))
    qpools = {q: [stack.enter_context(nc.semaphore(f"sem_{q}{i}")) for i in range(n)]
              for q, n in Prog.NPOOL.items()}
    block = stack.enter_context(nc.Block())

    def _wrap(eng):
        def em(fn, reads=(), writes=(), deps=(), inc=None):
            deps = list(deps)
            touched = []
            for k in tuple(reads) + tuple(writes):
                base = k if isinstance(k, str) else k[0]
                rg = key2reg.get(base)
                if rg is None:
                    continue
                r, g = rg
                if r.cur_group != g:
                    r.pending = tuple(r.last_handles.values())
                    r.absorbed = set()
                    r.last_handles = {}
                    r.cur_group = g
                if r.pending and eng not in r.absorbed:
                    deps.extend(r.pending)
                    r.absorbed.add(eng)
                touched.append(r)
            h = p.emit(eng, fn, reads=reads, writes=writes, deps=deps, inc=inc)
            hk = eng if h.pool is None else (eng, h.pool)
            for r in touched:
                r.last_handles[hk] = h
            return h
        return em

    qg, qs, te, ve, se = (_wrap(e) for e in ("qg", "qs", "tensor", "vector", "scalar"))

    def phase():
        pass

    qs(lambda e: e.dma_start(out=identf[:], in_=identf_in[:]), writes=["identf"])
    qg(lambda e: e.dma_start(out=identb[:], in_=identb_in[:]), writes=["identb"])
    qs(lambda e: e.dma_start(out=m1f[:], in_=m1f_in[:]), writes=["m1f"])
    qs(lambda e: e.dma_start(out=eye32[:], in_=eye32_in[:]), writes=["eye32"])
    qg(lambda e: e.dma_start(out=eye32b[:], in_=eye32_in[:]), writes=["eye32b"])
    qg(lambda e: e.dma_start(out=dbig[:], in_=dbig_in[:]), writes=["dbig"])
    qg(lambda e: e.dma_start(out=uf[:], in_=uf_in[:]), writes=["uf"])
    qg(lambda e: e.dma_start(out=gb[:], in_=gbrow[:]), writes=["gb"])
    qg(lambda e: e.dma_start(out=convb_b[:], in_=convb[:]), writes=["convb_b"])
    qs(lambda e: e.dma_start(out=alog_c[:], in_=alog[:]), writes=["alog_c"])
    qs(lambda e: e.dma_start(out=pa[:], in_=pacol[:]), writes=["pa"])
    qs(lambda e: e.dma_start(out=par[:], in_=parin[:]), writes=["par"])

    ve(lambda e: e.memset(onesf[:], 1.0), writes=["onesf"])
    ve(lambda e: e.memset(onesrow_b[:], 1.0), writes=["onesrow_b"])
    ve(lambda e: e.memset(onescol_b[:], 1.0), writes=["onescol_b"])
    qg(lambda e: e.dma_start(out=babuf[1:2, :], in_=ones512_in[:]), writes=["babuf1"])
    se(lambda e: e.activation(expal[:], alog_c[:], AF.Exp), reads=["alog_c"], writes=["expal"])
    ve(lambda e: e.tensor_single_scalar(nega[:], pa[:], -1.0, ALU.mult), reads=["pa"], writes=["nega"])
    ve(lambda e: e.tensor_scalar(ompar[:], par[:], -1.0, 1.0, ALU.mult, ALU.add),
       reads=["par"], writes=["ompar"])
    ve(lambda e: e.memset(lr1a[0:64, :], 0.0), writes=["lr1a"])
    ve(lambda e: e.memset(dr1a[:], 0.0), writes=["dr1a"])
    ve(lambda e: e.memset(s_out[0:64, :], 0.0), writes=["s_out"])
    ve(lambda e: e.memset(scoD0[:], 0.0), writes=["scoD0"])
    ve(lambda e: e.memset(scoD1[:], 0.0), writes=["scoD1"])

    def ln_colmath(pbS, kS, pbQ, kQ, n):
        ve(lambda e: e.tensor_single_scalar(murow[:], pbS[0:1, 0:512], 1.0 / n, ALU.mult),
           reads=[kS], writes=["murow"])
        ve(lambda e: e.tensor_single_scalar(vrow[:], pbQ[0:1, 0:512], 1.0 / n, ALU.mult),
           reads=[kQ], writes=["vrow"])
        ve(lambda e: e.tensor_tensor(browf[:], murow[:], murow[:], ALU.mult),
           reads=["murow"], writes=["browf"])
        ve(lambda e: e.tensor_tensor(vrow[:], vrow[:], browf[:], ALU.subtract),
           reads=["vrow", "browf"], writes=["vrow"])
        ve(lambda e: e.tensor_single_scalar(vrow[:], vrow[:], EPS, ALU.add),
           reads=["vrow"], writes=["vrow"])
        se(lambda e: e.activation(vrow[:], vrow[:], AF.Ln),
           reads=["vrow"], writes=["vrow"])
        se(lambda e: e.activation(vrow[:], vrow[:], AF.Exp, scale=-0.5),
           reads=["vrow"], writes=["vrow"])
        ve(lambda e: e.tensor_tensor(browf[:], murow[:], vrow[:], ALU.mult),
           reads=["murow", "vrow"], writes=["browf"])
        ve(lambda e: e.tensor_single_scalar(babuf[0:1, :], browf[:], -1.0, ALU.mult),
           reads=["browf"], writes=["babuf0"])
        ve(lambda e: e.tensor_copy(vrowb[:], vrow[:]), reads=["vrow"], writes=["vrowb"])

    def bcast64(col_t, col_key, dst, dst_key):
        ve(lambda e: e.tensor_single_scalar(diagt[:], eye32[:], col_t, ALU.mult),
           reads=["eye32", col_key], writes=["diagt"])
        te(lambda e: e.matmul(pb[2][0:64, 0:32], onesf[:, 0:64], diagt[:], start=True, stop=True),
           reads=["onesf", "diagt"], writes=[("pb", 2)])
        ve(lambda e: e.tensor_copy(dst[0:64, :], pb[2][0:64, 0:32]),
           reads=[("pb", 2)], writes=[dst_key])

    for t in range(min(NT, 99)):
        phase()
        for k in range(8):
            qg(lambda e, t=t, k=k: e.dma_start(out=ub[:, k * TK:(k + 1) * TK], in_=uin[t, k]),
               writes=[("ub", k)])
            qg(lambda e, k=k: e.dma_start(out=w1sb[:, k * DIN:(k + 1) * DIN], in_=w1t[k]),
               writes=[("w1sb", k)])
        if phases < 1:
            continue

        phase()
        wlist = [(0, 512), (512, 512), (1024, TK - 1024)]
        for f in range(18):
            M = 128 if f < 17 else 32
            bks = [pb[(f * 3 + wi) % 4] for wi in range(3)]
            pks = [("pb", (f * 3 + wi) % 4) for wi in range(3)]
            for k in range(8):
                for wi in range(2):
                    w0, wn = wlist[wi]
                    te(lambda e, f=f, k=k, w0=w0, wn=wn, M=M, bank=bks[wi]:
                       e.matmul(bank[0:M, 0:wn],
                                w1sb[:, k * DIN + f * 128: k * DIN + f * 128 + M],
                                ub[:, k * TK + w0: k * TK + w0 + wn],
                                start=(k == 0), stop=(k == 7)),
                       reads=[("w1sb", k), ("ub", k)], writes=[pks[wi]])
            for k in range(8):
                w0, wn = wlist[2]
                te(lambda e, f=f, k=k, w0=w0, wn=wn, M=M, bank=bks[2]:
                   e.matmul(bank[0:M, 0:wn],
                            w1sb[:, k * DIN + f * 128: k * DIN + f * 128 + M],
                            ub[:, k * TK + w0: k * TK + w0 + wn],
                            start=(k == 0), stop=(k == 7)),
                   reads=[("w1sb", k), ("ub", k)], writes=[pks[2]])
            for wi, (w0, wn) in enumerate(wlist):
                bank, pk = bks[wi], pks[wi]
                if f < 17:
                    se(lambda e, f=f, w0=w0, wn=wn, bank=bank:
                       e.activation(xbc[:, f * TK + w0: f * TK + w0 + wn],
                                    bank[:, 0:wn], AF.Identity),
                       reads=[pk], writes=[("xbc", f)])
                else:
                    se(lambda e, w0=w0, wn=wn, bank=bank:
                       e.activation(a_t[:, w0: w0 + wn], bank[0:32, 0:wn], AF.Sigmoid, scale=-1.0),
                       reads=[pk], writes=["a_t"])
                    ve(lambda e, w0=w0, wn=wn:
                       e.tensor_single_scalar(a_t[:, w0: w0 + wn], a_t[:, w0: w0 + wn],
                                              1e-37, ALU.max),
                       reads=["a_t"], writes=["a_t"])
                    se(lambda e, w0=w0, wn=wn:
                       e.activation(a_t[:, w0: w0 + wn], a_t[:, w0: w0 + wn], AF.Ln),
                       reads=["a_t"], writes=["a_t"])
                    ve(lambda e, w0=w0, wn=wn:
                       e.tensor_single_scalar(a_t[:, w0: w0 + wn], a_t[:, w0: w0 + wn],
                                              expal[:], ALU.mult),
                       reads=["a_t", "expal"], writes=["a_t"])
        if phases < 2:
            continue

        phase()
        for oc in range(17):
            wk = ("wcv", oc % 2)
            wbuf = wcv0 if oc % 2 == 0 else wcv1
            qg(lambda e, oc=oc, wbuf=wbuf: e.dma_start(out=wbuf[:], in_=wcvt[oc]), writes=[wk])
            groups = [(0, 128, oc)] if oc < 16 else [(0, 64, 16), (64, 64, 17)]
            for gi, (m0, M, dst) in enumerate(groups):
                b0 = ((oc + gi) % 2) * 2
                bks = (pb[b0], pb[b0 + 1])
                pks = (("pb", b0), ("pb", b0 + 1))
                for idx in range(68):
                    k, ic = idx // 17, idx % 17
                    for wi in range(2):
                        te(lambda e, k=k, ic=ic, wi=wi, m0=m0, M=M, idx=idx, bank=bks[wi], wbuf=wbuf:
                           e.matmul(bank[0:M, 0:512],
                                    wbuf[:, (k * 17 + ic) * 128 + m0:
                                         (k * 17 + ic) * 128 + m0 + M],
                                    xbc[:, ic * TK + wi * 512 + k: ic * TK + wi * 512 + k + 512],
                                    start=(idx == 0), stop=False),
                           reads=[wk, ("xbc", ic)], writes=[pks[wi]])
                for wi in range(2):
                    te(lambda e, oc=oc, m0=m0, M=M, bank=bks[wi]:
                       e.matmul(bank[0:M, 0:512],
                                convb_b[0:1, oc * 128 + m0: oc * 128 + m0 + M],
                                onesrow_b[0:1, 0:512], start=False, stop=True),
                       reads=["convb_b", "onesrow_b"], writes=[pks[wi]])
                for wi in range(2):
                    se(lambda e, M=M, wi=wi, bank=bks[wi]:
                       e.activation(rtmp[0:M, wi * 0: wi * 0 + 512] if False else rtmp[0:M, :],
                                    bank[0:M, :], AF.Relu),
                       reads=[pks[wi]], writes=["rtmp"])
                    se(lambda e, M=M, bank=bks[wi]:
                       e.activation(htmp[0:M, :], bank[0:M, :], AF.Relu, scale=-1.0),
                       reads=[pks[wi]], writes=["htmp"])
                    ve(lambda e, dst=dst, wi=wi, M=M:
                       e.scalar_tensor_tensor(cvo[0:M, dst * TILE + wi * 512: dst * TILE + wi * 512 + 512],
                                              htmp[0:M, :], nega[0:M, :], rtmp[0:M, :],
                                              ALU.mult, ALU.add),
                       reads=["htmp", "rtmp", "nega"], writes=[("cvo", dst, wi)])
        if phases < 3:
            continue

        phase()
        for k in range(8):
            qg(lambda e, k=k: e.dma_start(out=w2sb[:, k * DI:(k + 1) * DI], in_=w2t[k]),
               writes=[("w2sb", k)])
        for f in range(16):
            bks = [pb[(f % 2) * 2 + wi] for wi in range(2)]
            pks = [("pb", (f % 2) * 2 + wi) for wi in range(2)]
            for k in range(8):
                for wi in range(2):
                    w0 = wi * 512
                    te(lambda e, f=f, k=k, w0=w0, bank=bks[wi]:
                       e.matmul(bank[:, 0:512],
                                w2sb[:, k * DI + f * 128: k * DI + f * 128 + 128],
                                ub[:, k * TK + 3 + w0: k * TK + 3 + w0 + 512],
                                start=(k == 0), stop=(k == 7)),
                       reads=[("w2sb", k), ("ub", k)], writes=[pks[wi]])
            for wi in range(2):
                w0 = wi * 512
                se(lambda e, f=f, w0=w0, bank=bks[wi]:
                   e.activation(u2g[:, f * TILE + w0: f * TILE + w0 + 512], bank[:], AF.Gelu),
                   reads=[pks[wi]], writes=[("u2g", f)])
        if phases < 4:
            continue

        phase()
        ve(lambda e: e.memset(s_loc[0:64, :], 0.0), writes=["s_loc"])
        ve(lambda e: e.memset(offsc[:], 0.0), writes=["offsc"])

        for c in range(NCK):
            tok = c * LC
            wv = c // 2
            cp = c % 2
            csA, csB = csA_[cp], csB_[cp]
            negcstm, dcolC = negcstm_[cp], dcolC_[cp]
            bdall, msco = bdall_[cp], msco_[cp]
            kcsA, kcsB = ("csA", cp), ("csB", cp)
            kneg, kdcol = ("negcstm", cp), ("dcolC", cp)
            kbd, kms = ("bdall", cp), ("msco", cp)
            for sc in range(2):
                te(lambda e, tok=tok, sc=sc:
                   e.transpose(pb[5][0:128, 0:32],
                               a_t[:, 3 + tok + sc * 128: 3 + tok + sc * 128 + 128],
                               identf[0:32, 0:32]),
                   reads=["a_t", "identf"], writes=[("pb", 5)])
                ve(lambda e, sc=sc: e.tensor_copy(attm[:, sc * 32: sc * 32 + 32], pb[5][:, 0:32]),
                   reads=[("pb", 5)], writes=["attm"])
            for sc in range(2):
                te(lambda e, sc=sc:
                   e.matmul(pb[5][0:32, 0:256], attm[:, sc * 32: sc * 32 + 32],
                            m1f[:, sc * 256: sc * 256 + 256], start=(sc == 0), stop=(sc == 1)),
                   reads=["attm", "m1f"], writes=[("pb", 5)])
            se(lambda e: e.activation(csfm[:], pb[5][0:32, 0:256], AF.Identity),
               reads=[("pb", 5)], writes=["csfm"])
            se(lambda e, c=c: e.activation(ecs_all[:, c * 256:(c + 1) * 256], csfm[:], AF.Exp),
               reads=["csfm"], writes=["ecs_all"])
            for sc in range(2):
                te(lambda e, sc=sc:
                   e.transpose(pb[5][0:128, 0:32], csfm[:, sc * 128: sc * 128 + 128],
                               identf[0:32, 0:32]),
                   reads=["csfm", "identf"], writes=[("pb", 5)])
                ve(lambda e, sc=sc: e.tensor_copy(cstm[:, sc * 32: sc * 32 + 32], pb[5][:, 0:32]),
                   reads=[("pb", 5)], writes=["cstm"])
            ve(lambda e, negcstm=negcstm: e.tensor_single_scalar(negcstm[:], cstm[:], -1.0, ALU.mult),
               reads=["cstm"], writes=[kneg])
            ve(lambda e, csA=csA: e.tensor_copy(csA[:], csfm[:]), reads=["csfm"], writes=[kcsA])
            ve(lambda e, csA=csA, csB=csB: e.tensor_tensor(csB[:], csfm[:], csA[:], ALU.subtract),
               reads=["csfm", kcsA], writes=[kcsB])
            ve(lambda e, c=c: e.tensor_copy(offs_hist[:, c:c + 1], offsc[:]),
               reads=["offsc"], writes=["offs_hist"])
            ve(lambda e: e.tensor_tensor(offsc[:], offsc[:], csfm[:, 255:256], ALU.add),
               reads=["offsc", "csfm"], writes=["offsc"])
            se(lambda e: e.activation(echs[:], csfm[:, 255:256], AF.Exp),
               reads=["csfm"], writes=["echs"])
            bcast64(echs[:], "echs", dcolC, kdcol)
            ve(lambda e: e.tensor_single_scalar(diagt[:], eye32[:], csfm[:, 255:256], ALU.mult),
               reads=["eye32", "csfm"], writes=["diagt"])
            te(lambda e: e.matmul(pb[5][0:128, 0:32], onesf[:, 0:128], diagt[:],
                                  start=True, stop=True),
               reads=["onesf", "diagt"], writes=[("pb", 5)])
            for sc in range(2):
                ve(lambda e, sc=sc:
                   e.tensor_tensor(dtm_f[:, sc * 32: sc * 32 + 32], pb[5][:, 0:32],
                                   cstm[:, sc * 32: sc * 32 + 32], ALU.subtract),
                   reads=[("pb", 5), "cstm"], writes=["dtm_f"])
            se(lambda e: e.activation(dectm[:], dtm_f[:], AF.Exp),
               reads=["dtm_f"], writes=["dectm"])

            for f in range(16):
                for sc in range(2):
                    col = f * TILE + tok + sc * 128
                    ptb, ptk = pt[sc], ("pt", sc)
                    te(lambda e, col=col, ptb=ptb:
                       e.transpose(ptb[:, 0:128], cvo[:, col: col + 128], identb[:]),
                       reads=[("cvo", f, wv), "identb"], writes=[ptk])
                    ve(lambda e, col=col, ptb=ptb: e.tensor_copy(cvo[:, col: col + 128],
                                                                 ptb[:, 0:128]),
                       reads=[ptk], writes=[("cvo", f, wv)])
            for sc in range(2):
                col = 16 * TILE + tok + sc * 128
                te(lambda e, col=col: e.transpose(pt[0][0:128, 0:64], cvo[0:64, col: col + 128],
                                                  identb[0:64, 0:64]),
                   reads=[("cvo", 16, wv), "identb"], writes=[("pt", 0)])
                ve(lambda e, sc=sc: e.tensor_copy(btm[:, sc * 64: sc * 64 + 64], pt[0][:, 0:64]),
                   reads=[("pt", 0)], writes=["btm"])
            for sc in range(2):
                ve(lambda e, sc=sc, bdall=bdall:
                   e.tensor_tensor(
                       bdall[:, sc * 2048:(sc + 1) * 2048].rearrange("p (h n) -> p h n", n=64),
                       btm[:, sc * 64: sc * 64 + 64].unsqueeze(1).broadcast_to([128, 32, 64]),
                       dectm[:, sc * 32: sc * 32 + 32].unsqueeze(2).broadcast_to([128, 32, 64]),
                       ALU.mult),
                   reads=["btm", "dectm"], writes=[kbd])

            for sc in range(2):
                te(lambda e, tok=tok, sc=sc:
                   e.matmul(pb[5][:, sc * 256: sc * 256 + 256],
                            cvo[0:64, 16 * TILE + tok + sc * 128: 16 * TILE + tok + sc * 128 + 128],
                            cvo[0:64, 17 * TILE + tok: 17 * TILE + tok + 256],
                            start=True, stop=True),
                   reads=[("cvo", 16, wv), ("cvo", 17, wv)], writes=[("pb", 5)])
            ve(lambda e, msco=msco: e.tensor_copy(msco[:], pb[5][:, 0:512]),
               reads=[("pb", 5)], writes=[kms])

            for j in range(16):
                par2 = j % 2
                pY = pb[4]
                pYk = ("pb", 4)
                bA, kA = pb[0 + 2 * par2], ("pb", 0 + 2 * par2)
                bB, kB = pb[1 + 2 * par2], ("pb", 1 + 2 * par2)
                scoD = scoD0 if par2 == 0 else scoD1
                skD = "scoD0" if par2 == 0 else "scoD1"
                ctl, kctl = ctl_[par2], ("ctl", par2)
                for hh in range(2):
                    h = 2 * j + hh
                    for sc, l0, ln in ((0, 0, 256), (1, 128, 128)):
                        bank, kk = (bA, kA) if sc == 0 else (bB, kB)
                        te(lambda e, h=h, hh=hh, l0=l0, ln=ln, bank=bank, csA=csA:
                           e.matmul(bank[:, hh * 256: hh * 256 + ln],
                                    eye32b[:, h: h + 1].broadcast_to([32, 128]),
                                    csA[:, l0: l0 + ln], start=True, stop=False),
                           reads=["eye32b", kcsA], writes=[kk])
                        te(lambda e, h=h, hh=hh, l0=l0, ln=ln, bank=bank, csB=csB:
                           e.matmul(bank[:, hh * 256: hh * 256 + ln],
                                    eye32b[:, h: h + 1].broadcast_to([32, 128]),
                                    csB[:, l0: l0 + ln], start=False, stop=False),
                           reads=["eye32b", kcsB], writes=[kk])
                        te(lambda e, sc=sc, hh=hh, l0=l0, ln=ln, bank=bank:
                           e.matmul(bank[:, hh * 256: hh * 256 + ln], dbig[:],
                                    uf[:, sc * 256 + l0: sc * 256 + l0 + ln],
                                    start=False, stop=True),
                           reads=["dbig", "uf"], writes=[kk])
                for hh in range(2):
                    h = 2 * j + hh
                    for sc, l0, ln in ((0, 0, 256), (1, 128, 128)):
                        bank, kk = (bA, kA) if sc == 0 else (bB, kB)
                        se(lambda e, h=h, hh=hh, sc=sc, l0=l0, ln=ln, bank=bank, scoD=scoD, negcstm=negcstm:
                           e.activation(scoD[:, hh * 512 + sc * 256 + l0:
                                             hh * 512 + sc * 256 + l0 + ln],
                                        bank[:, hh * 256: hh * 256 + ln], AF.Exp,
                                        bias=negcstm[:, sc * 32 + h: sc * 32 + h + 1]),
                           reads=[kk, kneg], writes=[skD])
                ve(lambda e, scoD=scoD, msco=msco:
                   e.tensor_tensor(scoD[:].rearrange("p (h x) -> p h x", h=2),
                                   scoD[:].rearrange("p (h x) -> p h x", h=2),
                                   msco[:].unsqueeze(1).broadcast_to([128, 2, 512]),
                                   ALU.mult),
                   reads=[skD, kms], writes=[skD])
                for hh in range(2):
                    h = 2 * j + hh
                    te(lambda e, h=h, c=c, bB=bB:
                       e.matmul(bB[0:64, 0:256],
                                eye32b[:, h: h + 1].broadcast_to([32, 64]),
                                ecs_all[:, c * 256:(c + 1) * 256], start=True, stop=True),
                       reads=["eye32b", "ecs_all"], writes=[kB])
                    ve(lambda e, hh=hh, tok=tok, bB=bB, ctl=ctl:
                       e.tensor_tensor(ctl[0:64, hh * 256: hh * 256 + 256],
                                       cvo[0:64, 17 * TILE + tok: 17 * TILE + tok + 256],
                                       bB[0:64, 0:256], ALU.mult),
                       reads=[("cvo", 17, wv), kB], writes=[kctl])
                for hh in range(2):
                    h = 2 * j + hh
                    xcol = (h // 2) * TILE + tok
                    for sc, l0, ln in ((0, 0, 256), (1, 128, 128)):
                        te(lambda e, h=h, hh=hh, sc=sc, l0=l0, ln=ln, xcol=xcol, pY=pY, scoD=scoD:
                           e.matmul(pY[hh * 64: hh * 64 + 64, l0: l0 + ln],
                                    cvo[:, xcol + sc * 128 + (h % 2) * 64:
                                        xcol + sc * 128 + (h % 2) * 64 + 64],
                                    scoD[:, hh * 512 + sc * 256 + l0:
                                         hh * 512 + sc * 256 + l0 + ln],
                                    start=(sc == 0), stop=False),
                           reads=[("cvo", h // 2, wv), skD], writes=[pYk])
                    te(lambda e, h=h, hh=hh, pY=pY, ctl=ctl:
                       e.matmul(pY[hh * 64: hh * 64 + 64, 0:256],
                                s_loc[0:64, h * 64: h * 64 + 64],
                                ctl[0:64, hh * 256: hh * 256 + 256], start=False, stop=True),
                       reads=["s_loc", kctl], writes=[pYk])
                for hh in range(2):
                    h = 2 * j + hh
                    xcol = (h // 2) * TILE + tok
                    for sc in range(2):
                        te(lambda e, h=h, hh=hh, sc=sc, xcol=xcol, pY=pY, bdall=bdall:
                           e.matmul(pY[0:64, 256 + hh * 64: 256 + hh * 64 + 64],
                                    bdall[:, sc * 2048 + h * 64: sc * 2048 + h * 64 + 64],
                                    cvo[:, xcol + sc * 128 + (h % 2) * 64:
                                        xcol + sc * 128 + (h % 2) * 64 + 64],
                                    start=(sc == 0), stop=(sc == 1)),
                           reads=[kbd, ("cvo", h // 2, wv)], writes=[pYk])
                for hh in range(2):
                    h = 2 * j + hh
                    ve(lambda e, h=h, hh=hh, pY=pY, dcolC=dcolC:
                       e.scalar_tensor_tensor(s_loc[0:64, h * 64: h * 64 + 64],
                                              s_loc[0:64, h * 64: h * 64 + 64],
                                              dcolC[0:64, h: h + 1],
                                              pY[0:64, 256 + hh * 64: 256 + hh * 64 + 64],
                                              ALU.mult, ALU.add),
                       reads=[kdcol, pYk, "s_loc"], writes=["s_loc"])
                ve(lambda e, j=j, tok=tok, pY=pY:
                   e.tensor_copy(yb[:, j * TILE + tok: j * TILE + tok + 256], pY[:, 0:256]),
                   reads=[pYk], writes=[("yb", j)])
        if phases < 5:
            continue

        phase()
        lr1_cur, lr1_prev = (lr1b, lr1a) if t % 2 == 0 else (lr1a, lr1b)
        dr1_cur, dr1_prev = (dr1b, dr1a) if t % 2 == 0 else (dr1a, dr1b)
        k_lr1c, k_lr1p = ("lr1b", "lr1a") if t % 2 == 0 else ("lr1a", "lr1b")
        k_dr1c, k_dr1p = ("dr1b", "dr1a") if t % 2 == 0 else ("dr1a", "dr1b")

        se(lambda e: e.activation(offs_hist[:, NCK:NCK + 1], offsc[:], AF.Exp),
           reads=["offsc"], writes=["offs_hist"])
        qg(lambda e, t=t: e.dma_start(out=cc_in[t][0:CCL].rearrange("(p n) -> p n", p=64),
                                      in_=s_loc[0:64, :]),
           reads=["s_loc"], writes=[("ccin", t)])
        qg(lambda e, t=t: e.dma_start(out=cc_in[t][CCL:CCN].rearrange("(p n) -> p n", p=32),
                                      in_=offs_hist[:, NCK:NCK + 1]),
           reads=["offs_hist"], writes=[("ccin2", t)])
        p.emit("qg", lambda e, t=t: e.collective_compute(
            "AllGather", ALU.bypass, ins=[cc_in[t][:]], outs=[cc_out[t][:]],
            replica_groups=RG),
            reads=[("ccin", t), ("ccin2", t)], writes=[("ccout", t)], inc=1)
        qg(lambda e, t=t: e.dma_start(
            out=lr0[0:64, :], in_=cc_out[t][0, 0:CCL].rearrange("(p n) -> p n", p=64)),
            reads=[("ccout", t)], writes=["lr0"])
        qg(lambda e, t=t, lr1_cur=lr1_cur: e.dma_start(
            out=lr1_cur[0:64, :], in_=cc_out[t][1, 0:CCL].rearrange("(p n) -> p n", p=64)),
            reads=[("ccout", t)], writes=[k_lr1c])
        qg(lambda e, t=t: e.dma_start(
            out=dr0[:], in_=cc_out[t][0, CCL:CCN].rearrange("(p n) -> p n", p=32)),
            reads=[("ccout", t)], writes=["dr0"])
        qg(lambda e, t=t, dr1_cur=dr1_cur: e.dma_start(
            out=dr1_cur[:], in_=cc_out[t][1, CCL:CCN].rearrange("(p n) -> p n", p=32)),
            reads=[("ccout", t)], writes=[k_dr1c])

        ve(lambda e: e.tensor_single_scalar(dmixc[:], dr0[:], par[0:32, :], ALU.mult),
           reads=["dr0", "par"], writes=["dmixc"])
        ve(lambda e, dr1_prev=dr1_prev:
           e.scalar_tensor_tensor(dmixc[:], dr1_prev[:], ompar[0:32, :], dmixc[:],
                                  ALU.mult, ALU.add),
           reads=[k_dr1p, "ompar", "dmixc"], writes=["dmixc"])
        ve(lambda e: e.tensor_single_scalar(lr0[0:64, :], lr0[0:64, :], par[0:64, :],
                                            ALU.mult),
           reads=["lr0", "par"], writes=["lr0"])
        ve(lambda e, lr1_prev=lr1_prev:
           e.scalar_tensor_tensor(lr0[0:64, :], lr1_prev[0:64, :], ompar[0:64, :],
                                  lr0[0:64, :], ALU.mult, ALU.add),
           reads=[k_lr1p, "ompar", "lr0"], writes=["lr0"])
        bcast64(dmixc[:], "dmixc", dmixcol, "dmixcol")
        bcast64(offs_hist[:, NCK:NCK + 1], "offs_hist", dtilecol, "dtilecol")
        for h in range(NH):
            ve(lambda e, h=h:
               e.scalar_tensor_tensor(lr0[0:64, h * 64: h * 64 + 64],
                                      s_out[0:64, h * 64: h * 64 + 64],
                                      dmixcol[0:64, h: h + 1],
                                      lr0[0:64, h * 64: h * 64 + 64], ALU.mult, ALU.add),
               reads=["s_out", "dmixcol", "lr0"], writes=["lr0"])
        for h in range(NH):
            ve(lambda e, h=h:
               e.scalar_tensor_tensor(s_out[0:64, h * 64: h * 64 + 64],
                                      lr0[0:64, h * 64: h * 64 + 64],
                                      dtilecol[0:64, h: h + 1],
                                      s_loc[0:64, h * 64: h * 64 + 64], ALU.mult, ALU.add),
               reads=["lr0", "dtilecol", "s_loc"], writes=["s_out"])

        phase()
        for c in range(NCK):
            tok = c * LC
            se(lambda e, c=c: e.activation(echs[:], offs_hist[:, c:c + 1], AF.Exp),
               reads=["offs_hist"], writes=["echs"])
            bcast64(echs[:], "echs", eoffcol, "eoffcol")
            wv = c // 2
            for j in range(16):
                pY = pb[5] if j % 2 == 0 else pb[4]
                pYk = ("pb", 5) if j % 2 == 0 else ("pb", 4)
                ctl, kctl = ctl_[j % 2], ("ctl", j % 2)
                for hh in range(2):
                    h = 2 * j + hh
                    ve(lambda e, h=h, hh=hh:
                       e.tensor_single_scalar(scorr[0:64, hh * 64: hh * 64 + 64],
                                              lr0[0:64, h * 64: h * 64 + 64],
                                              eoffcol[0:64, h: h + 1], ALU.mult),
                       reads=["lr0", "eoffcol"], writes=["scorr"])
                    te(lambda e, h=h, c=c:
                       e.matmul(pb[3][0:64, 0:256],
                                eye32b[:, h: h + 1].broadcast_to([32, 64]),
                                ecs_all[:, c * 256:(c + 1) * 256], start=True, stop=True),
                       reads=["eye32b", "ecs_all"], writes=[("pb", 3)])
                    ve(lambda e, hh=hh, tok=tok, ctl=ctl:
                       e.tensor_tensor(ctl[0:64, hh * 256: hh * 256 + 256],
                                       cvo[0:64, 17 * TILE + tok: 17 * TILE + tok + 256],
                                       pb[3][0:64, 0:256], ALU.mult),
                       reads=[("cvo", 17, wv), ("pb", 3)], writes=[kctl])
                for hh in range(2):
                    te(lambda e, hh=hh, pY=pY, ctl=ctl:
                       e.matmul(pY[hh * 64: hh * 64 + 64, 0:256],
                                scorr[0:64, hh * 64: hh * 64 + 64],
                                ctl[0:64, hh * 256: hh * 256 + 256], start=True, stop=True),
                       reads=["scorr", kctl], writes=[pYk])
                ve(lambda e, j=j, tok=tok, pY=pY:
                   e.tensor_tensor(lnt[:, 0:256], pY[:, 0:256],
                                   yb[:, j * TILE + tok: j * TILE + tok + 256], ALU.add),
                   reads=[pYk, ("yb", j)], writes=["lnt"])
                ve(lambda e, j=j, tok=tok:
                   e.tensor_tensor(yb[:, j * TILE + tok: j * TILE + tok + 256], lnt[:, 0:256],
                                   u2g[:, j * TILE + tok: j * TILE + tok + 256], ALU.mult),
                   reads=["lnt", ("u2g", j)], writes=[("yb", j)])
        if phases < 6:
            continue

        phase()
        for k in range(16):
            qg(lambda e, k=k: e.dma_start(out=w3sb[:, k * DM:(k + 1) * DM], in_=w3t[k]),
               writes=[("w3sb", k)])
        for wi in range(2):
            w0 = wi * 512
            for f in range(16):
                se(lambda e, f=f, w0=w0:
                   e.activation(htmp[:], yb[:, f * TILE + w0: f * TILE + w0 + 512], AF.Square),
                   reads=[("yb", f)], writes=["htmp"])
                te(lambda e, f=f, w0=w0:
                   e.matmul(pb[0][0:1, 0:512], onescol_b[:, 0:1],
                            yb[:, f * TILE + w0: f * TILE + w0 + 512],
                            start=(f == 0), stop=(f == 15)),
                   reads=[("yb", f), "onescol_b"], writes=[("pb", 0)])
                te(lambda e, f=f:
                   e.matmul(pb[1][0:1, 0:512], onescol_b[:, 0:1], htmp[:],
                            start=(f == 0), stop=(f == 15)),
                   reads=["htmp", "onescol_b"], writes=[("pb", 1)])
            ln_colmath(pb[0], ("pb", 0), pb[1], ("pb", 1), DI)
            for f in range(16):
                te(lambda e, f=f: e.matmul(pb[2][:, 0:512], gb[0:1, f * 128: f * 128 + 128],
                                           vrowb[0:1, 0:512], start=True, stop=True),
                   reads=["gb", "vrowb"], writes=[("pb", 2)])
                te(lambda e, f=f: e.matmul(pb[3][:, 0:512], gb[:, f * 128: f * 128 + 128],
                                           babuf[:, 0:512], start=True, stop=True),
                   reads=["gb", "babuf0", "babuf1"], writes=[("pb", 3)])
                ve(lambda e: e.tensor_copy(abt[:], pb[2][:, 0:512]),
                   reads=[("pb", 2)], writes=["abt"])
                ve(lambda e: e.tensor_copy(bbt[:], pb[3][:, 0:512]),
                   reads=[("pb", 3)], writes=["bbt"])
                ve(lambda e, f=f, w0=w0:
                   e.tensor_tensor(lnt[:], yb[:, f * TILE + w0: f * TILE + w0 + 512], abt[:],
                                   ALU.mult),
                   reads=[("yb", f), "abt"], writes=["lnt"])
                ve(lambda e, f=f, w0=w0:
                   e.tensor_tensor(yb[:, f * TILE + w0: f * TILE + w0 + 512], lnt[:], bbt[:],
                                   ALU.add),
                   reads=["lnt", "bbt"], writes=[("yb", f)])

            for d in range(8):
                bank, pk = pb[4 + (d % 2)], ("pb", 4 + (d % 2))
                for k in range(16):
                    te(lambda e, d=d, k=k, w0=w0, bank=bank:
                       e.matmul(bank[:, 0:512], w3sb[:, k * DM + d * 128: k * DM + d * 128 + 128],
                                yb[:, k * TILE + w0: k * TILE + w0 + 512],
                                start=(k == 0), stop=(k == 15)),
                       reads=[("w3sb", k), ("yb", k)], writes=[pk])
                ve(lambda e, d=d, w0=w0, bank=bank:
                   e.tensor_tensor(xpre[:, d * 512: d * 512 + 512], bank[:, 0:512],
                                   ub[:, d * TK + 3 + w0: d * TK + 3 + w0 + 512], ALU.add),
                   reads=[pk, ("ub", d)], writes=[("xpre", d)])
            for d in range(8):
                se(lambda e, d=d: e.activation(htmp[:], xpre[:, d * 512: d * 512 + 512], AF.Square),
                   reads=[("xpre", d)], writes=["htmp"])
                te(lambda e, d=d: e.matmul(pb[0][0:1, 0:512], onescol_b[:, 0:1],
                                           xpre[:, d * 512: d * 512 + 512],
                                           start=(d == 0), stop=(d == 7)),
                   reads=[("xpre", d), "onescol_b"], writes=[("pb", 0)])
                te(lambda e, d=d: e.matmul(pb[1][0:1, 0:512], onescol_b[:, 0:1], htmp[:],
                                           start=(d == 0), stop=(d == 7)),
                   reads=["htmp", "onescol_b"], writes=[("pb", 1)])
            ln_colmath(pb[0], ("pb", 0), pb[1], ("pb", 1), DM)
            te(lambda e: e.matmul(pb[2][:, 0:512], onesrow_b[0:1, 0:128], vrowb[0:1, 0:512],
                                  start=True, stop=True),
               reads=["onesrow_b", "vrowb"], writes=[("pb", 2)])
            te(lambda e: e.matmul(pb[3][:, 0:512], onesrow_b[0:1, 0:128], babuf[0:1, 0:512],
                                  start=True, stop=True),
               reads=["onesrow_b", "babuf0"], writes=[("pb", 3)])
            ve(lambda e: e.tensor_copy(abt[:], pb[2][:, 0:512]), reads=[("pb", 2)], writes=["abt"])
            ve(lambda e: e.tensor_copy(bbt[:], pb[3][:, 0:512]), reads=[("pb", 3)], writes=["bbt"])
            for d in range(8):
                ve(lambda e, d=d:
                   e.tensor_tensor(lnt[:], xpre[:, d * 512: d * 512 + 512], abt[:], ALU.mult),
                   reads=[("xpre", d), "abt"], writes=["lnt"])
                ve(lambda e, d=d, w0=w0:
                   e.tensor_tensor(x1b[:, d * TILE + w0: d * TILE + w0 + 512], lnt[:], bbt[:],
                                   ALU.add),
                   reads=["lnt", "bbt"], writes=[("x1b", d)])
        if phases < 7:
            continue

        phase()
        for hi in range(2):
            ht0 = hi * 512
            for j in range(32):
                slot = j % 2
                qg(lambda e, j=j, slot=slot:
                   e.dma_start(out=wfcsl[:, slot * 1024: slot * 1024 + 1024], in_=wfct[j]),
                   writes=[("wfc", slot)])
                bank, pk = pb[(2 + j) % 4 if False else (j % 2)], ("pb", j % 2)
                for k in range(8):
                    te(lambda e, j=j, k=k, slot=slot, ht0=ht0, bank=bank:
                       e.matmul(bank[:, 0:512],
                                wfcsl[:, slot * 1024 + k * 128: slot * 1024 + k * 128 + 128],
                                x1b[:, k * TILE + ht0: k * TILE + ht0 + 512],
                                start=(k == 0), stop=(k == 7)),
                       reads=[("wfc", slot), ("x1b", k)], writes=[pk])
                se(lambda e, bank=bank: e.activation(rtmp[:], bank[:], AF.Relu),
                   reads=[pk], writes=["rtmp"])
                ve(lambda e, j=j: e.tensor_tensor(mid_sb[:, j * FW: j * FW + FW], rtmp[:],
                                                  rtmp[:], ALU.mult),
                   reads=["rtmp"], writes=[("mid", j)])
            for dq in range(4):
                for j in range(32):
                    slot = j % 2
                    qg(lambda e, j=j, dq=dq, slot=slot:
                       e.dma_start(out=wpjsl[:, slot * 256: slot * 256 + 256],
                                   in_=wpjt[j, :, dq * 256: dq * 256 + 256]),
                       writes=[("wpj", slot)])
                    for dd in range(2):
                        for w2i in range(2):
                            bk = 2 + w2i * 2 + dd
                            te(lambda e, j=j, slot=slot, w2i=w2i, dd=dd, bk=bk:
                               e.matmul(pb[bk][:, 0:256],
                                        wpjsl[:, slot * 256 + dd * 128: slot * 256 + dd * 128 + 128],
                                        mid_sb[:, j * FW + w2i * 256: j * FW + w2i * 256 + 256],
                                        start=(j == 0), stop=(j == 31)),
                               reads=[("wpj", slot), ("mid", j)], writes=[("pb", bk)])
                for w2i in range(2):
                    for dd in range(2):
                        d = dq * 2 + dd
                        bk = 2 + w2i * 2 + dd
                        ve(lambda e, d=d, bk=bk, w2i=w2i:
                           e.tensor_copy(y_sb[:, d * FW + w2i * 256: d * FW + w2i * 256 + 256],
                                         pb[bk][:, 0:256]),
                           reads=[("pb", bk)], writes=[("y", d)])
            yk = [("y", d) for d in range(8)]
            ve(lambda e: e.tensor_copy(memb[:], y_sb[:]), reads=yk, writes=["memb"])
            ve(lambda e: e.tensor_single_scalar(accb[:], y_sb[:], 1.0, ALU.is_gt),
               reads=yk, writes=["accb"])
            ve(lambda e: e.tensor_single_scalar(y_sb[:], y_sb[:], 1.0, ALU.subtract),
               reads=yk, writes=yk)
            for it in range(4):
                ve(lambda e: e.scalar_tensor_tensor(tmpb[:], memb[:], 0.9, y_sb[:],
                                                    ALU.mult, ALU.add),
                   reads=["memb"] + yk, writes=["tmpb"])
                ve(lambda e: e.scalar_tensor_tensor(memb[:], memb[:], 1.0, tmpb[:],
                                                    ALU.is_le, ALU.add),
                   reads=["memb", "tmpb"], writes=["memb"])
                ve(lambda e: e.scalar_tensor_tensor(accb[:], memb[:], 1.0, accb[:],
                                                    ALU.is_gt, ALU.add),
                   reads=["memb", "accb"], writes=["accb"])
            for d in range(8):
                ve(lambda e, d=d, ht0=ht0:
                   e.scalar_tensor_tensor(tmpb[:, d * FW: d * FW + FW],
                                          accb[:, d * FW: d * FW + FW], 0.2,
                                          x1b[:, d * TILE + ht0: d * TILE + ht0 + 512],
                                          ALU.mult, ALU.add),
                   reads=["accb", ("x1b", d)], writes=["tmpb"])
            for tg in range(4):
                for d in range(8):
                    te(lambda e, d=d, tg=tg:
                       e.transpose(pb[0][:, 0:128], tmpb[:, d * FW + tg * 128: d * FW + tg * 128 + 128],
                                   identf[:]),
                       reads=["tmpb", "identf"], writes=[("pb", 0)])
                    ve(lambda e, d=d: e.tensor_copy(x2tm[:, d * 128: d * 128 + 128],
                                                    pb[0][:, 0:128]),
                       reads=[("pb", 0)], writes=["x2tm"])
                ve(lambda e: e.reduce_sum(mucol[:], x2tm[:], axis=AXX),
                   reads=["x2tm"], writes=["mucol"])
                se(lambda e: e.activation(sqtm[:], x2tm[:], AF.Square),
                   reads=["x2tm"], writes=["sqtm"])
                ve(lambda e: e.reduce_sum(m2col[:], sqtm[:], axis=AXX),
                   reads=["sqtm"], writes=["m2col"])
                ve(lambda e: e.tensor_single_scalar(mucol[:], mucol[:], 1.0 / DM, ALU.mult),
                   reads=["mucol"], writes=["mucol"])
                ve(lambda e: e.tensor_single_scalar(m2col[:], m2col[:], 1.0 / DM, ALU.mult),
                   reads=["m2col"], writes=["m2col"])
                ve(lambda e: e.tensor_tensor(rscol[:], mucol[:], mucol[:], ALU.mult),
                   reads=["mucol"], writes=["rscol"])
                ve(lambda e: e.tensor_tensor(m2col[:], m2col[:], rscol[:], ALU.subtract),
                   reads=["m2col", "rscol"], writes=["m2col"])
                ve(lambda e: e.tensor_single_scalar(m2col[:], m2col[:], EPS, ALU.add),
                   reads=["m2col"], writes=["m2col"])
                se(lambda e: e.activation(m2col[:], m2col[:], AF.Ln),
                   reads=["m2col"], writes=["m2col"])
                se(lambda e: e.activation(rscol[:], m2col[:], AF.Exp, scale=-0.5),
                   reads=["m2col"], writes=["rscol"])
                ve(lambda e: e.tensor_scalar(obuf[:], x2tm[:], mucol[:], rscol[:],
                                             ALU.subtract, ALU.mult),
                   reads=["x2tm", "mucol", "rscol"], writes=["obuf"])
                qs(lambda e, t=t, hi=hi, tg=tg:
                   e.dma_start(out=out[t, hi * 512 + tg * 128: hi * 512 + tg * 128 + 128, :],
                               in_=obuf[:]),
                   reads=["obuf"])

    bar = p.barrier_handles()
    for name in dumps:
        tsr = sb[name]
        ap0 = tsr[:]
        od = nc.dram_tensor("dbg_" + name, list(ap0.shape), F32, kind="ExternalOutput").ap()
        qg(lambda e, tsr=tsr, od=od: e.dma_start(out=od[:], in_=tsr[:]), deps=bar)

    p.run(block, sems, qpools)
    stack.close()
    return nc


def prep_shared(inputs):
    f = np.ascontiguousarray
    W1 = inputs["L0_attn_W_in1"]
    w1t = f(np.asarray(W1)[DI:DI + DIN, :].T).reshape(8, 128, DIN)
    cw = np.asarray(inputs["L0_attn_conv_w"])
    cwt = cw.reshape(17, 128, 17, 128, KS)
    cwt = np.transpose(cwt, (0, 3, 4, 2, 1))
    wcvt = f(cwt).reshape(17, 128, 68 * 128)
    w2t = f(np.asarray(inputs["L0_attn_W_in2"]).T).reshape(8, 128, DI)
    w3t = f(np.asarray(inputs["L0_attn_W_in3"]).T).reshape(16, 128, DM)
    Wfc = np.asarray(inputs["L0_ffn_W_fc"])
    wfct = f(np.transpose(Wfc.reshape(32, 128, 8, 128), (0, 3, 2, 1))).reshape(32, 128, 1024)
    Wpj = np.asarray(inputs["L0_ffn_W_proj"])
    wpjt = f(Wpj.T).reshape(32, 128, DM)

    m1 = np.tril(np.ones((256, 256), np.float32)).T
    m1f = f(np.transpose(m1.reshape(2, 128, 256), (1, 0, 2))).reshape(128, 512)

    return {
        "w1t": w1t.astype(np.float32),
        "wcvt": wcvt.astype(np.float32),
        "w2t": w2t.astype(np.float32),
        "w3t": w3t.astype(np.float32),
        "wfct": wfct.astype(np.float32),
        "wpjt": wpjt.astype(np.float32),
        "alog": np.asarray(inputs["L0_attn_A_log"]).reshape(32, 1).astype(np.float32),
        "convb": np.asarray(inputs["L0_attn_conv_b"]).reshape(1, CH).astype(np.float32),
        "gbrow": np.stack([np.asarray(inputs["L0_attn_ln_g"]),
                           np.asarray(inputs["L0_attn_ln_b"])]).astype(np.float32),
        "pacol": np.full((128, 1), np.asarray(inputs["L0_attn_prelu_a"]), np.float32),
        "identf_in": np.eye(128, dtype=np.float32),
        "identb_in": np.eye(128, dtype=np.float32),
        "m1f_in": m1f.astype(np.float32),
        "eye32_in": np.eye(32, dtype=np.float32),
        "ones512_in": np.ones((1, 512), np.float32),
        "dbig_in": (-65536.0 * np.eye(128)).astype(np.float32),
        "uf_in": (1.0 - m1f).astype(np.float32),
    }


def prep_core(u, core, ntiles):
    b, parity = core // 2, core % 2
    ut = np.ascontiguousarray(np.asarray(u)[b].T)
    tiles = []
    for tt in range(ntiles):
        g = 2 * tt + parity
        t0 = g * TILE
        if t0 == 0:
            seg = np.concatenate([np.zeros((DM, 3), np.float32), ut[:, 0:TILE]], axis=1)
        else:
            seg = ut[:, t0 - 3: t0 + TILE]
        tiles.append(seg)
    arr = np.stack(tiles).reshape(ntiles, 8, 128, TK)
    return np.ascontiguousarray(arr.astype(np.float32))


def kernel(**inputs):
    NT = 4
    shared = prep_shared(inputs)
    u = inputs["L0_attn_u"]
    in_maps = []
    for core in range(8):
        m = dict(shared)
        m["uin"] = prep_core(u, core, NT)
        m["parin"] = np.full((128, 1), float(core % 2), np.float32)
        in_maps.append(m)
    nc = build_program(ntiles=NT, dumps=(), phases=99)
    res = run_bass_kernel_spmd(nc, in_maps, core_ids=list(range(8)))
    y = np.empty((NB, SEQ, DM), np.float32)
    for core in range(8):
        b, parity = core // 2, core % 2
        o = res.results[core]["out"]
        for tt in range(NT):
            g = 2 * tt + parity
            y[b, g * TILE:(g + 1) * TILE, :] = o[tt]
    return y
